# version 20
# baseline (speedup 1.0000x reference)
import contextlib
import numpy as np
import ml_dtypes
import concourse.bass as bass
import concourse.mybir as mybir
from concourse.bass_utils import run_bass_kernel_spmd
from concourse.aot_env import allow_ici_all_to_all

F32 = mybir.dt.float32
F16 = mybir.dt.float16
I32 = mybir.dt.int32
AT = mybir.AluOpType
AF = mybir.ActivationFunctionType

NC = 8
TLOC = 2048
TEXT = 2176
NTILE = 17
D = 1024
INNER = 2048
NST = 16
FF = 4096
EPS = 1e-5
SSM_NT = [512, 512, 512, 512, 128]

def _grp_tiles(p):
    return 5 if p % 2 == 0 else 4

CUMT = [0]
for _p in range(32):
    CUMT.append(CUMT[-1] + _grp_tiles(_p))
def _sp1_ssm(k):
    return CUMT[2 * k] + 5
def _sp1_res(k):
    return CUMT[2 * k + 2]

SA_A_TOTAL = 1 + NTILE * 4


def build_nc(debug=False):
    nc = bass.Bass("TRN2", target_bir_lowering=False)

    xs = nc.dram_tensor("xs", [TEXT, D], F32, kind="ExternalInput").ap()
    wint = nc.dram_tensor("wint", [D, 2 * INNER], F32, kind="ExternalInput").ap()
    woutt = nc.dram_tensor("woutt", [INNER, D], F32, kind="ExternalInput").ap()
    convw = nc.dram_tensor("convw", [128, 64], F32, kind="ExternalInput").ap()
    convb = nc.dram_tensor("convb", [128, 16], F32, kind="ExternalInput").ap()
    alog = nc.dram_tensor("alog", [128, 2 * NST], F32, kind="ExternalInput").ap()
    dvec = nc.dram_tensor("dvec", [128, 16], F32, kind="ExternalInput").ap()
    gam = nc.dram_tensor("gam", [128, D], F32, kind="ExternalInput").ap()
    bet = nc.dram_tensor("bet", [128, D], F32, kind="ExternalInput").ap()
    q1 = nc.dram_tensor("q1", [D, FF], I32, kind="ExternalInput").ap()
    s1r = nc.dram_tensor("s1r", [D, FF], F32, kind="ExternalInput").ap()
    b1 = nc.dram_tensor("b1", [128, 32], F32, kind="ExternalInput").ap()
    q2 = nc.dram_tensor("q2", [FF, D], I32, kind="ExternalInput").ap()
    s2r = nc.dram_tensor("s2r", [FF, D], F32, kind="ExternalInput").ap()
    b2 = nc.dram_tensor("b2", [128, 8], F32, kind="ExternalInput").ap()
    eye = nc.dram_tensor("eye", [128, 128], F16, kind="ExternalInput").ap()
    zz = nc.dram_tensor("zz", [128, 1], F32, kind="ExternalInput").ap()
    cst = nc.dram_tensor("cst", [128, 2], F32, kind="ExternalInput").ap()
    out = nc.dram_tensor("out", [TLOC, D], F32, kind="ExternalOutput").ap()
    dbg = {}
    if debug:
        for nm, shp, dt in [("dbg_snd1", [INNER, TLOC], F16),
                            ("dbg_rcv2", [INNER, TLOC], F16),
                            ("dbg_y1", [FF, TLOC], F16),
                            ("dbg_u", [INNER, TLOC], F16)]:
            dbg[nm] = nc.dram_tensor(nm, shp, dt, kind="ExternalOutput").ap()

    snd1 = nc.dram_tensor("snd1", [INNER, TLOC], F16)
    rcv1 = nc.dram_tensor("rcv1", [INNER, TLOC], F16)
    snd2 = nc.dram_tensor("snd2", [INNER, TLOC], F16)
    rcv2 = nc.dram_tensor("rcv2", [INNER, TLOC], F16)
    uscr = nc.dram_tensor("uscr", [INNER, TLOC], F16)
    y1scr = nc.dram_tensor("y1scr", [FF, TLOC], F16)
    yscr = nc.dram_tensor("yscr", [INNER, TLOC], F16)

    with contextlib.ExitStack() as _st:
        def _sb(name, shape, dt):
            return _st.enter_context(nc.sbuf_tensor(name, shape, dt))

        def _sem(name):
            return _st.enter_context(nc.semaphore(name))

        eye_sb = _sb("eye_sb", [128, 128], F16)
        zz_sb = _sb("zz_sb", [128, 1], F32)
        convw_sb = _sb("convw_sb", [128, 64], F32)
        convb_sb = _sb("convb_sb", [128, 16], F32)
        alog_sb = _sb("alog_sb", [128, 2 * NST], F32)
        a32 = _sb("a32", [128, 2 * NST], F32)
        tmpA = _sb("tmpA", [128, 2 * NST], F32)
        dvec_sb = _sb("dvec_sb", [128, 16], F32)
        b1_sb = _sb("b1_sb", [128, 32], F32)
        b2_sb = _sb("b2_sb", [128, 8], F32)
        r1all = _sb("r1all", [128, NTILE], F32)
        nmr1all = _sb("nmr1all", [128, NTILE], F32)
        r2all = _sb("r2all", [128, NTILE], F32)
        nmr2all = _sb("nmr2all", [128, NTILE], F32)
        st1all = _sb("st1all", [128, 2 * NTILE], F32)
        st2all = _sb("st2all", [128, 2 * NTILE], F32)
        sq_all = _sb("sq_all", [128, NTILE], F32)
        tmp_all = _sb("tmp_all", [128, NTILE], F32)
        stat_scr = _sb("stat_scr", [128, 12], F32)
        r1p1 = _sb("r1p1", [128, NTILE], F32)
        c128 = _sb("c128", [128, 1024], F32)
        cst_sb = _sb("cst_sb", [128, 2], F32)
        nc.const_aps.register(F32, EPS, cst_sb[:, 0:1])
        nc.const_aps.register(F32, 1.0, cst_sb[:, 1:2])
        (dx, gcst, vA, sA, tA, eA, gwb, tP1, sP1, sXCV, gC, gd, gq, gW1, tF1,
         sF1, y1d, cc1, bar, dB, sB, sAD, vS, tSG, sHS, gd2, dY, gq2, gW2,
         tF2, sZ, gZ, cc2, dC, vY, gdY, dYs, gWo, tW, sW, tTr, sTr, vOut,
         dOut, vCV, dxC, sA2, gGB) = [_sem(n) for n in (
            "dx", "gcst", "vA", "sA", "tA", "eA", "gwb", "tP1", "sP1", "sXCV",
            "gC", "gd", "gq", "gW1", "tF1", "sF1", "y1d", "cc1", "bar", "dB",
            "sB", "sAD", "vS", "tSG", "sHS", "gd2", "dY", "gq2", "gW2", "tF2",
            "sZ", "gZ", "cc2", "dC", "vY", "gdY", "dYs", "gWo", "tW", "sW",
            "tTr", "sTr", "vOut", "dOut", "vCV", "dxC", "sA2", "gGB")]
        gdS = [_sem("gdS0"), _sem("gdS1")]
        y1S = [_sem("y1S0"), _sem("y1S1"), _sem("y1S2"), _sem("y1S3")]
        gd2S = [_sem("gd2S0"), _sem("gd2S1")]
        dBS = [_sem("dBS0"), _sem("dBS1"), _sem("dBS2"), _sem("dBS3")]
        dYS2 = [_sem("dYS20"), _sem("dYS21")]
        dCS = [_sem("dCS0"), _sem("dCS1")]
        dYsS = [_sem("dYsS0"), _sem("dYsS1")]
        dxCS = [_sem("dxCS0"), _sem("dxCS1")]
        gdYS = [_sem("gdYS0"), _sem("gdYS1")]
        gqS = [_sem("gqS0"), _sem("gqS1")]
        sCR = _sem("sCR")
        gq2S = [_sem("gq2S0"), _sem("gq2S1")]
        dOutS = [_sem("dOutS0"), _sem("dOutS1")]

        _stA = contextlib.ExitStack()

        def _sbA(name, shape, dt):
            return _stA.enter_context(nc.sbuf_tensor(name, shape, dt))

        def _psA(name, shape, dt):
            return _stA.enter_context(nc.psum_tensor(name, shape, dt))

        xt0 = _sbA("xt0", [128, D], F32)
        xt1 = _sbA("xt1", [128, D], F32)
        xn32 = _sbA("xn32", [128, D], F32)
        xn16r = _sbA("xn16r", [128, 8 * D], F16)
        hf16 = _sbA("hf16", [128, 2 * D], F16)
        gam16 = _sbA("gam16", [128, D], F16)
        bet16 = _sbA("bet16", [128, D], F16)
        xnT = _sbA("xnT", [128, 8 * TEXT], F16)
        hfT = _sbA("hfT", [128, 8 * TEXT], F16)
        wb0 = _sbA("wb0", [128, 8 * 128], F16)
        wb1 = _sbA("wb1", [128, 8 * 128], F16)
        wb2 = _sbA("wb2", [128, 8 * 128], F16)
        wb3 = _sbA("wb3", [128, 8 * 128], F16)
        xssm0 = _sbA("xssm0", [128, TEXT], F16)
        xssm1 = _sbA("xssm1", [128, TEXT], F16)
        cacc0 = _sbA("cacc0", [128, TLOC], F16)
        cacc1 = _sbA("cacc1", [128, TLOC], F16)
        xcv16 = _sbA("xcv16", [128, 2 * TLOC], F16)
        sres16 = _sbA("sres16", [128, 2 * TLOC], F16)
        u16t = _sbA("u16t", [128, TLOC], F16)
        qraw = _sbA("qraw", [128, 2 * 1024], F32)
        srep = _sbA("srep", [128, 2 * 1024], F32)
        w1q = _sbA("w1q", [128, 2 * 8 * 1024], F16)
        y1t = _sbA("y1t", [128, 4 * 512], F16)
        psA0 = _psA("psA0", [128, 512], F32)
        psA1 = _psA("psA1", [128, 512], F32)
        psC0 = _psA("psC0", [128, 512], F32)
        psC1 = _psA("psC1", [128, 512], F32)
        psC2 = _psA("psC2", [128, 512], F32)
        psC3 = _psA("psC3", [128, 512], F32)
        psT0 = _psA("psT0", [128, 1024], F16)
        psT1 = _psA("psT1", [128, 1024], F16)
        with nc.Block("A") as blk:
            psA = [psA0, psA1]
            psC = [psC0, psC1, psC2, psC3]
            xtile = [xt0, xt1]
            xssm = [xssm0, xssm1]
            cacc = [cacc0, cacc1]
            wb = [wb0, wb1, wb2, wb3]

            def a_sync(e):
                e.dma_start(out=alog_sb[:], in_=alog).then_inc(gcst, 16)
                e.dma_start(out=cst_sb[:], in_=cst).then_inc(gcst, 16)
                e.dma_start(out=eye_sb[:], in_=eye).then_inc(gcst, 16)
                e.dma_start(out=zz_sb[:], in_=zz).then_inc(gcst, 16)
                e.dma_start(out=convw_sb[:], in_=convw).then_inc(gcst, 16)
                e.dma_start(out=convb_sb[:], in_=convb).then_inc(gcst, 16)
                e.dma_start(out=b1_sb[:], in_=b1).then_inc(gcst, 16)
                e.dma_start(out=b2_sb[:], in_=b2).then_inc(gcst, 16)
                for tt in range(NTILE):
                    if tt >= 2:
                        e.wait_ge(vA, tt - 1)
                    e.dma_start(out=xtile[tt % 2][:],
                                in_=xs[128 * tt:128 * (tt + 1), :]).then_inc(dx, 16)
                for tt in range(NTILE):
                    e.wait_ge(vA, 18 if tt < 2 else 18 + tt)
                    e.dma_start(out=xtile[tt % 2][:],
                                in_=xs[128 * tt:128 * (tt + 1), :]).then_inc(dx, 16)
                for tt in range(NTILE):
                    e.wait_ge(vA, 38 if tt < 2 else 38 + tt - 1)
                    e.dma_start(out=xtile[tt % 2][:],
                                in_=xs[128 * tt:128 * (tt + 1), :]).then_inc(dx, 16)
                for j in range(128):
                    q, ft, fc = j // 32, (j // 8) % 4, j % 8
                    e.wait_ge(sF1, j + 1)
                    e.dma_start(out=y1scr.ap()[1024 * q + 128 * fc:1024 * q + 128 * (fc + 1),
                                               512 * ft:512 * (ft + 1)],
                                in_=y1t[:, (j % 4) * 512:(j % 4 + 1) * 512]).then_inc(y1S[j % 4], 16)
                for _sl in range(4):
                    e.wait_ge(y1S[_sl], 16 * 32)
                e.nop().then_inc(bar, 1)

            def a_vector(v):
                v.wait_ge(sA2, 1)
                v.tensor_scalar(out=a32[:], in0=tmpA[:], scalar1=-1.0, scalar2=None,
                                op0=AT.mult)
                v.drain()
                for tt in range(NTILE):
                    v.wait_ge(dx, 16 * min(tt + 2, NTILE))
                    xt = xtile[tt % 2]
                    v.bn_stats(out=stat_scr[:, 0:6], in_=xt[:, 0:512])
                    v.bn_stats(out=stat_scr[:, 6:12], in_=xt[:, 512:1024])
                    v.drain()
                    v.bn_aggr(out=st1all[:, 2 * tt:2 * tt + 2], in_=stat_scr[:, 0:12])
                    v.drain()
                    v.engine_nop().then_inc(vA, 1)
                v.drain()
                v.engine_nop().then_inc(vA, 1)
                v.wait_ge(sA, 1)
                v.reciprocal(out=r1all[:], in_=sq_all[:])
                v.drain()
                v.tensor_tensor(out=tmp_all[:], in0=st1all.ap().rearrange("p (t two) -> p t two", two=2)[:, :, 0],
                                in1=r1all[:], op=AT.mult)
                v.drain()
                v.tensor_scalar(out=nmr1all[:], in0=tmp_all[:], scalar1=-1.0,
                                scalar2=None, op0=AT.mult)
                v.drain()
                v.engine_nop().then_inc(vA, 1)
                for tt in range(NTILE):
                    v.wait_ge(dx, 16 * (NTILE + min(tt + 2, NTILE)))
                    if tt >= 8:
                        v.wait_ge(tA, tt - 7)
                    xt = xtile[tt % 2]
                    v.tensor_scalar(out=xn32[:], in0=xt[:], scalar1=r1all[:, tt:tt + 1],
                                    scalar2=nmr1all[:, tt:tt + 1], op0=AT.mult, op1=AT.add)
                    v.drain()
                    v.tensor_copy(out=xn16r[:, (tt % 8) * D:(tt % 8 + 1) * D], in_=xn32[:])
                    v.bn_stats(out=stat_scr[:, 0:6], in_=xn32[:, 0:512])
                    v.bn_stats(out=stat_scr[:, 6:12], in_=xn32[:, 512:1024])
                    v.drain()
                    v.bn_aggr(out=st2all[:, 2 * tt:2 * tt + 2], in_=stat_scr[:, 0:12])
                    v.drain()
                    v.engine_nop().then_inc(vA, 1)
                v.drain()
                v.engine_nop().then_inc(vA, 1)
                v.wait_ge(sA, 2)
                v.reciprocal(out=r2all[:], in_=sq_all[:])
                v.drain()
                v.tensor_tensor(out=tmp_all[:], in0=st2all.ap().rearrange("p (t two) -> p t two", two=2)[:, :, 0],
                                in1=r2all[:], op=AT.mult)
                v.drain()
                v.tensor_scalar(out=nmr2all[:], in0=tmp_all[:], scalar1=-1.0,
                                scalar2=None, op0=AT.mult)
                v.drain()
                v.engine_nop().then_inc(vA, 1)
                v.wait_ge(gcst, 16 * 8)
                v.wait_ge(gGB, 32)
                for tt in range(NTILE):
                    v.wait_ge(dx, 16 * (2 * NTILE + min(tt + 2, NTILE)))
                    if tt >= 2:
                        v.wait_ge(tA, NTILE + tt - 1)
                    v.tensor_scalar(out=xn32[:], in0=xtile[tt % 2][:],
                                    scalar1=r1all[:, tt:tt + 1],
                                    scalar2=nmr1all[:, tt:tt + 1], op0=AT.mult, op1=AT.add)
                    v.drain()
                    hfp = hf16[:, (tt % 2) * D:(tt % 2 + 1) * D]
                    v.tensor_scalar(out=hfp, in0=xn32[:],
                                    scalar1=r2all[:, tt:tt + 1],
                                    scalar2=nmr2all[:, tt:tt + 1], op0=AT.mult, op1=AT.add)
                    v.drain()
                    v.tensor_tensor(out=hfp, in0=hfp, in1=gam16[:], op=AT.mult)
                    v.drain()
                    v.tensor_tensor(out=hfp, in0=hfp, in1=bet16[:], op=AT.add)
                    v.drain()
                    v.engine_nop().then_inc(vA, 1)
                for k in range(16):
                    v.wait_ge(sP1, _sp1_ssm(k))
                    if k >= 2:
                        v.wait_ge(sXCV, k - 1)
                    xsk = xssm[k % 2]
                    v.tensor_scalar(out=cacc[k % 2][:], in0=xsk[:, 125:125 + TLOC],
                                    scalar1=convw_sb[:, 4 * k:4 * k + 1],
                                    scalar2=None, op0=AT.mult)
                    for tau in range(1, 4):
                        v.drain()
                        v.scalar_tensor_tensor(out=cacc[k % 2][:],
                                               in0=xsk[:, 125 + tau:125 + tau + TLOC],
                                               scalar=convw_sb[:, 4 * k + tau:4 * k + tau + 1],
                                               in1=cacc[k % 2][:], op0=AT.mult, op1=AT.add)
                    v.drain()
                    v.engine_nop().then_inc(vCV, 1)
                v.engine_nop().then_inc(bar, 1)

            def a_scalar(s):
                s.wait_ge(gcst, 16 * 8)
                s.activation(out=tmpA[:], in_=alog_sb[:], func=AF.Exp).then_inc(sA2, 1)
                s.wait_ge(vA, 18)
                s.activation(out=sq_all[:],
                             in_=st1all.ap().rearrange("p (t two) -> p t two", two=2)[:, :, 1],
                             func=AF.Sqrt, bias=EPS).then_inc(sA, 1)
                for tt in range(NTILE):
                    s.wait_ge(tA, tt + 1)
                    for dd in range(8):
                        s.copy(out=xnT[:, dd * TEXT + 128 * tt:dd * TEXT + 128 * (tt + 1)],
                               in_=psT0[:, 128 * dd:128 * (dd + 1)])
                    s.drain()
                    s.nop().then_inc(eA, 1)
                s.wait_ge(vA, 37)
                s.activation(out=sq_all[:],
                             in_=st2all.ap().rearrange("p (t two) -> p t two", two=2)[:, :, 1],
                             func=AF.Sqrt, bias=EPS).then_inc(sA, 1)
                for tt in range(NTILE):
                    s.wait_ge(tA, NTILE + tt + 1)
                    for dd in range(8):
                        s.copy(out=hfT[:, dd * TEXT + 128 * tt:dd * TEXT + 128 * (tt + 1)],
                               in_=psT1[:, 128 * dd:128 * (dd + 1)])
                    s.drain()
                    s.nop().then_inc(eA, 1)
                for k in range(16):
                    for nt in range(5):
                        s.wait_ge(tP1, CUMT[2 * k] + nt + 1)
                        if k >= 2 and nt == 0:
                            s.wait_ge(vCV, k - 1)
                        n = SSM_NT[nt]
                        s.copy(out=xssm[k % 2][:, 512 * nt:512 * nt + n],
                               in_=psA[(CUMT[2 * k] + nt) % 2][:, 0:n]).then_inc(sP1, 1)
                    s.wait_ge(vCV, k + 1)
                    if k >= 2:
                        s.wait_ge(gdS[k % 2], 32 * (k // 2))
                    s.activation(out=xcv16[:, (k % 2) * TLOC:(k % 2 + 1) * TLOC],
                                 in_=cacc[k % 2][:], func=AF.Silu,
                                 bias=convb_sb[:, k:k + 1]).then_inc(sXCV, 1)
                    for nt in range(4):
                        s.wait_ge(tP1, CUMT[2 * k + 1] + nt + 1)
                        if k >= 1 and nt == 0:
                            s.wait_ge(gC, k)
                        s.activation(out=sres16[:, (k % 2) * TLOC + 512 * nt:(k % 2) * TLOC + 512 * (nt + 1)],
                                     in_=psA[(CUMT[2 * k + 1] + nt) % 2][:, :],
                                     func=AF.Silu).then_inc(sP1, 1)
                for j in range(128):
                    q, fc = j // 32, j % 8
                    s.wait_ge(tF1, j + 1)
                    if j >= 4:
                        s.wait_ge(y1S[j % 4], 16 * (j // 4))
                    fcg = q * 8 + fc
                    s.activation(out=y1t[:, (j % 4) * 512:(j % 4 + 1) * 512],
                                 in_=psC[j % 4][:, :], func=AF.Silu,
                                 bias=b1_sb[:, fcg:fcg + 1]).then_inc(sF1, 1)
                s.nop().then_inc(bar, 1)

            def a_tensor(t):
                t.wait_ge(gcst, 16 * 8)
                for tt in range(NTILE):
                    t.wait_ge(vA, 19 + tt + 1)
                    if tt >= 1:
                        t.wait_ge(eA, tt)
                    for dd in range(8):
                        t.transpose(out=psT0[:, 128 * dd:128 * (dd + 1)],
                                    in_=xn16r[:, (tt % 8) * D + 128 * dd:(tt % 8) * D + 128 * (dd + 1)],
                                    identity=eye_sb[:])
                    t.drain()
                    t.nop().then_inc(tA, 1)
                for tt in range(NTILE):
                    t.wait_ge(vA, 38 + tt + 1)
                    if tt >= 1:
                        t.wait_ge(eA, NTILE + tt)
                    for dd in range(8):
                        t.transpose(out=psT1[:, 128 * dd:128 * (dd + 1)],
                                    in_=hf16[:, (tt % 2) * D + 128 * dd:(tt % 2) * D + 128 * (dd + 1)],
                                    identity=eye_sb[:])
                    t.drain()
                    t.nop().then_inc(tA, 1)
                t.wait_ge(eA, NTILE)
                for p in range(32):
                    is_ssm = (p % 2 == 0)
                    t.wait_ge(gwb, 16 * min(2 * (p // 2) + 4, 32))
                    ntl = SSM_NT if is_ssm else [512, 512, 512, 512]
                    for nt, n in enumerate(ntl):
                        gidx = CUMT[p] + nt + 1
                        if gidx > 2:
                            t.wait_ge(sP1, gidx - 2)
                        col0 = (nt * 512) if is_ssm else (128 + nt * 512)
                        for dd in range(8):
                            ins = t.matmul(out=psA[(CUMT[p] + nt) % 2][:, 0:n],
                                           lhsT=wb[p % 4][:, 128 * dd:128 * (dd + 1)],
                                           rhs=xnT[:, dd * TEXT + col0:dd * TEXT + col0 + n],
                                           start=(dd == 0), stop=(dd == 7))
                        ins.then_inc(tP1, 1)
                t.wait_ge(eA, 2 * NTILE)
                for j in range(128):
                    q, ft, fc = j // 32, (j // 8) % 4, j % 8
                    if j % 32 == 0:
                        t.wait_ge(gW1, q + 1)
                    if j >= 4:
                        t.wait_ge(sF1, j - 3)
                    for kk in range(8):
                        ins = t.matmul(out=psC[j % 4][:, :],
                                       lhsT=w1q[:, (q % 2) * 8192 + kk * 1024 + fc * 128:(q % 2) * 8192 + kk * 1024 + (fc + 1) * 128],
                                       rhs=hfT[:, kk * TEXT + 128 + ft * 512:kk * TEXT + 128 + (ft + 1) * 512],
                                       start=(kk == 0), stop=(kk == 7))
                    ins.then_inc(tF1, 1)
                t.nop().then_inc(bar, 1)

            def a_gpsimd(g):
                g.memset(c128[:], 128.0)
                g.drain()
                g.dma_start(out=gam16[:], in_=gam).then_inc(gGB, 16)
                g.dma_start(out=bet16[:], in_=bet).then_inc(gGB, 16)

                def dequant_fc1(q):
                    if q >= 2:
                        g.wait_ge(tF1, 32 * (q - 1))
                    for kk in range(8):
                        sl = kk % 2
                        if kk == 0:
                            for pre in range(2):
                                g.dma_start(out=qraw[:, pre * 1024:(pre + 1) * 1024],
                                            in_=q1[128 * pre:128 * (pre + 1), 1024 * q:1024 * (q + 1)]).then_inc(gqS[pre], 16)
                                g.dma_start(out=srep[:, pre * 1024:(pre + 1) * 1024],
                                            in_=s1r[128 * pre:128 * (pre + 1), 1024 * q:1024 * (q + 1)]).then_inc(gqS[pre], 16)
                        g.wait_ge(gqS[sl], 32 * (q * 4 + kk // 2 + 1))
                        qr = qraw[:, sl * 1024:(sl + 1) * 1024]
                        g.tensor_tensor(out=qr, in0=qr, in1=c128[:], op=AT.subtract)
                        g.drain()
                        g.tensor_tensor(out=w1q[:, (q % 2) * 8192 + kk * 1024:(q % 2) * 8192 + (kk + 1) * 1024],
                                        in0=qr, in1=srep[:, sl * 1024:(sl + 1) * 1024], op=AT.mult)
                        g.drain()
                        if kk + 2 < 8:
                            g.dma_start(out=qraw[:, sl * 1024:(sl + 1) * 1024],
                                        in_=q1[128 * (kk + 2):128 * (kk + 3), 1024 * q:1024 * (q + 1)]).then_inc(gqS[sl], 16)
                            g.dma_start(out=srep[:, sl * 1024:(sl + 1) * 1024],
                                        in_=s1r[128 * (kk + 2):128 * (kk + 3), 1024 * q:1024 * (q + 1)]).then_inc(gqS[sl], 16)
                    g.nop().then_inc(gW1, 1)

                dequant_fc1(0)
                dequant_fc1(1)
                def wload(p):
                    k = p // 2
                    col0 = 128 * k if p % 2 == 0 else INNER + 128 * k
                    if p >= 4:
                        g.wait_ge(tP1, CUMT[p - 3])
                    g.dma_start(
                        out=wb[p % 4].ap().rearrange("p (dc i) -> p dc i", dc=8),
                        in_=wint[:, col0:col0 + 128].rearrange("(dc p) i -> p dc i", p=128),
                    ).then_inc(gwb, 16)

                for p in range(4):
                    wload(p)
                for k in range(16):
                    if 2 * k + 4 < 32:
                        wload(2 * k + 4)
                    if 2 * k + 5 < 32:
                        wload(2 * k + 5)
                    g.wait_ge(sXCV, k + 1)
                    g.wait_ge(sP1, _sp1_res(k))
                    if k >= 1:
                        g.wait_ge(gdS[(k - 1) % 2], 32 * ((k - 1) // 2 + 1))
                    g.tensor_tensor(out=u16t[:],
                                    in0=xcv16[:, (k % 2) * TLOC:(k % 2 + 1) * TLOC],
                                    in1=sres16[:, (k % 2) * TLOC:(k % 2 + 1) * TLOC],
                                    op=AT.mult)
                    g.drain()
                    g.nop().then_inc(gC, 1)
                    g.dma_start(out=snd1.ap()[128 * k:128 * (k + 1), :],
                                in_=xcv16[:, (k % 2) * TLOC:(k % 2 + 1) * TLOC]).then_inc(gdS[k % 2], 16)
                    g.dma_start(out=uscr.ap()[128 * k:128 * (k + 1), :],
                                in_=u16t[:]).then_inc(gdS[k % 2], 16)
                dequant_fc1(2)
                dequant_fc1(3)
                g.wait_ge(gdS[0], 32 * 8)
                g.wait_ge(gdS[1], 32 * 8)
                with allow_ici_all_to_all():
                    g.collective_compute(
                        "AllToAll", AT.bypass, replica_groups=[list(range(NC))],
                        ins=[snd1.ap().opt()], outs=[rcv1.ap().opt()],
                    ).then_inc(cc1, 1)
                g.nop().then_inc(bar, 1)

            blk.sync(a_sync)
            blk.vector(a_vector)
            blk.scalar(a_scalar)
            blk.tensor(a_tensor)
            blk.gpsimd(a_gpsimd)

        _stA.close()

        _stZ = contextlib.ExitStack()
        z16 = _stZ.enter_context(nc.sbuf_tensor("z16", [128, 8 * 2048], F16))

        _stB = contextlib.ExitStack()

        def _sbB(name, shape, dt):
            return _stB.enter_context(nc.sbuf_tensor(name, shape, dt))

        def _psB(name, shape, dt):
            return _stB.enter_context(nc.psum_tensor(name, shape, dt))

        xcvs = _sbB("xcvs", [128, 4 * 2048], F16)
        e32 = _sbB("e32", [128, 2048], F32)
        delta32 = _sbB("delta32", [128, 2048], F32)
        bx16 = _sbB("bx16", [128, 4 * 2048], F16)
        ad16 = _sbB("ad16", [128, 4 * 2048], F16)
        hring = _sbB("hring", [128, 8 * 2048], F16)
        carry = _sbB("carry", [128, 32], F16)
        hs16 = _sbB("hs16", [128, 2 * 1024], F16)
        ztmp = _sbB("ztmp", [128, 4 * 512], F16)
        w2q = _sbB("w2q", [128, 8 * 1024], F16)
        y1s0 = _sbB("y1s0", [128, 8 * 512], F16)
        y1s1 = _sbB("y1s1", [128, 8 * 512], F16)
        qraw2 = _sbB("qraw2", [128, 2 * 1024], F32)
        srep2 = _sbB("srep2", [128, 2 * 1024], F32)
        psS0 = _psB("psS0", [128, 1024], F32)
        psS1 = _psB("psS1", [128, 1024], F32)
        psF0 = _psB("psF0", [128, 512], F32)
        psF1 = _psB("psF1", [128, 512], F32)
        psF2 = _psB("psF2", [128, 512], F32)
        psF3 = _psB("psF3", [128, 512], F32)
        with nc.Block("B") as blk:
            psS = [psS0, psS1]
            psF = [psF0, psF1, psF2, psF3]
            y1s = [y1s0, y1s1]

            def rcv1_slice(gi2):
                k2, cc = gi2 // 2, gi2 % 2
                r0 = 256 * k2 + 128 * cc
                return rcv1.ap()[r0:r0 + 128, :]

            def snd2_slice2(gi2, half):
                k2, cc = gi2 // 2, gi2 % 2
                r0 = 256 * k2 + 128 * cc
                c0 = 1024 * half
                return snd2.ap()[r0:r0 + 128, c0:c0 + 1024]

            def b_sync(e):
                e.wait_ge(bar, 5)
                for gi2 in range(16):
                    if gi2 == 0:
                        e.wait_ge(cc1, 1)
                    if gi2 >= 4:
                        e.wait_ge(sB, gi2 - 3)
                    e.dma_start(out=xcvs[:, (gi2 % 4) * 2048:(gi2 % 4 + 1) * 2048],
                                in_=rcv1_slice(gi2)).then_inc(dBS[gi2 % 4], 16)
                e.nop().then_inc(bar, 1)

            def b_scalar(s):
                s.wait_ge(bar, 5)
                for gi2 in range(16):
                    cc = gi2 % 2
                    xc = xcvs[:, (gi2 % 4) * 2048:(gi2 % 4 + 1) * 2048]
                    s.wait_ge(dBS[gi2 % 4], 16 * (gi2 // 4 + 1))
                    s.activation(out=e32[:], in_=xc, func=AF.Exp)
                    if gi2 >= 4:
                        s.wait_ge(vS, 16 * (gi2 - 3))
                    s.activation(out=bx16[:, (gi2 % 4) * 2048:(gi2 % 4 + 1) * 2048],
                                 in_=xc, func=AF.Square).then_inc(sB, 1)
                    s.drain()
                    s.activation(out=delta32[:], in_=e32[:], func=AF.Ln, bias=1.0)
                    s.drain()
                    for n in range(NST):
                        idx = 16 * gi2 + n
                        w = idx - 3
                        if w >= 1:
                            s.wait_ge(vS, w)
                        s.activation(out=ad16[:, (n % 4) * 2048:(n % 4 + 1) * 2048],
                                     in_=delta32[:], func=AF.Exp,
                                     scale=a32[:, cc * NST + n:cc * NST + n + 1]).then_inc(sAD, 1)
                        if idx >= 8:
                            pidx = idx - 8
                            pch = ((pidx // 16) % 2) * 16 + (pidx % 16)
                            s.wait_ge(vS, pidx + 1)
                            s.copy(out=carry[:, pch:pch + 1],
                                   in_=hring[:, (pidx % 8) * 2048 + 2047:(pidx % 8) * 2048 + 2048]).then_inc(sCR, 1)
                        if n == 11 and gi2 >= 1:
                            j = gi2 - 1
                            s.wait_ge(tSG, 16 * gi2)
                            for half in range(2):
                                if j >= 1:
                                    s.wait_ge(gd2S[half], 16 * j)
                                s.copy(out=hs16[:, half * 1024:(half + 1) * 1024],
                                       in_=psS[half][:, :]).then_inc(sHS, 1)
                        if 1 <= gi2 <= 8:
                            zj = 16 * (gi2 - 1) + n
                            q, ft, oc = zj // 32, (zj // 8) % 4, zj % 8
                            if zj % 4 == 0:
                                s.wait_ge(tF2, zj + 4)
                            if q == 0:
                                s.copy(out=z16[:, oc * 2048 + ft * 512:oc * 2048 + (ft + 1) * 512],
                                       in_=psF[zj % 4][:, :]).then_inc(sZ, 1)
                            else:
                                m = zj - 32 + 1
                                if m >= 5:
                                    s.wait_ge(gZ, m - 4)
                                s.activation(out=ztmp[:, (zj % 4) * 512:(zj % 4 + 1) * 512],
                                             in_=psF[zj % 4][:, :],
                                             func=AF.Copy).then_inc(sZ, 1)
                for pidx in range(248, 256):
                    pch = ((pidx // 16) % 2) * 16 + (pidx % 16)
                    s.wait_ge(vS, pidx + 1)
                    s.copy(out=carry[:, pch:pch + 1],
                           in_=hring[:, (pidx % 8) * 2048 + 2047:(pidx % 8) * 2048 + 2048]).then_inc(sCR, 1)
                s.wait_ge(tSG, 256)
                for half in range(2):
                    s.wait_ge(gd2S[half], 16 * 15)
                    s.copy(out=hs16[:, half * 1024:(half + 1) * 1024],
                           in_=psS[half][:, :]).then_inc(sHS, 1)
                s.nop().then_inc(bar, 1)

            def b_vector(v):
                v.wait_ge(bar, 5)
                for gi2 in range(16):
                    cc = gi2 % 2
                    v.wait_ge(sB, gi2 + 1)
                    for n in range(NST):
                        idx = 16 * gi2 + n
                        ch = cc * 16 + n
                        v.wait_ge(sAD, idx + 1)
                        if idx >= 8:
                            v.wait_ge(tSG, idx - 7)
                            v.wait_ge(sCR, idx - 7)
                        hb = hring[:, (idx % 8) * 2048:(idx % 8 + 1) * 2048]
                        init = zz_sb[:, 0:1] if gi2 < 2 else carry[:, ch:ch + 1]
                        v.tensor_tensor_scan(out=hb,
                                             data0=ad16[:, (n % 4) * 2048:(n % 4 + 1) * 2048],
                                             data1=bx16[:, (gi2 % 4) * 2048:(gi2 % 4 + 1) * 2048],
                                             initial=init, op0=AT.mult,
                                             op1=AT.add).then_inc(vS, 1)
                v.engine_nop().then_inc(bar, 1)

            def b_tensor(t):
                t.wait_ge(bar, 5)

                def fc2_jy(jy):
                    q, ft = jy // 4, jy % 4
                    t.wait_ge(dYS2[jy % 2], 16 * (jy // 2 + 1))
                    if jy % 4 == 0:
                        t.wait_ge(gW2, q + 1)
                    for oc in range(8):
                        zj = jy * 8 + oc
                        if zj >= 4:
                            t.wait_ge(sZ, zj - 3)
                        for fc in range(8):
                            ins = t.matmul(out=psF[zj % 4][:, :],
                                           lhsT=w2q[:, fc * 1024 + oc * 128:fc * 1024 + (oc + 1) * 128],
                                           rhs=y1s[jy % 2][:, fc * 512:(fc + 1) * 512],
                                           start=(fc == 0), stop=(fc == 7))
                        ins.then_inc(tF2, 1)

                for gi2 in range(16):
                    if 1 <= gi2 <= 8:
                        fc2_jy(2 * (gi2 - 1))
                        fc2_jy(2 * (gi2 - 1) + 1)
                    t.wait_ge(sHS, 2 * gi2)
                    for n in range(NST):
                        idx = 16 * gi2 + n
                        t.wait_ge(vS, idx + 1)
                        hb = hring[:, (idx % 8) * 2048:(idx % 8 + 1) * 2048]
                        t.matmul(out=psS0[:, 0:512], lhsT=eye_sb[:], rhs=hb[:, 0:512],
                                 start=(n == 0), stop=(n == 15))
                        t.matmul(out=psS0[:, 512:1024], lhsT=eye_sb[:], rhs=hb[:, 512:1024],
                                 start=(n == 0), stop=(n == 15))
                        t.matmul(out=psS1[:, 0:512], lhsT=eye_sb[:], rhs=hb[:, 1024:1536],
                                 start=(n == 0), stop=(n == 15))
                        ins = t.matmul(out=psS1[:, 512:1024], lhsT=eye_sb[:], rhs=hb[:, 1536:2048],
                                       start=(n == 0), stop=(n == 15))
                        ins.then_inc(tSG, 1)
                t.nop().then_inc(bar, 1)

            def b_gpsimd(g):
                g.wait_ge(bar, 5)

                def dequant_quarter(q):
                    if q >= 1:
                        g.wait_ge(tF2, 32 * q)
                    for kk in range(8):
                        sl = kk % 2
                        if kk == 0:
                            for pre in range(2):
                                g.dma_start(out=qraw2[:, pre * 1024:(pre + 1) * 1024],
                                            in_=q2[1024 * q + 128 * pre:1024 * q + 128 * (pre + 1), :]).then_inc(gq2S[pre], 16)
                                g.dma_start(out=srep2[:, pre * 1024:(pre + 1) * 1024],
                                            in_=s2r[1024 * q + 128 * pre:1024 * q + 128 * (pre + 1), :]).then_inc(gq2S[pre], 16)
                        g.wait_ge(gq2S[sl], 32 * (q * 4 + kk // 2 + 1))
                        qr = qraw2[:, sl * 1024:(sl + 1) * 1024]
                        g.tensor_tensor(out=qr, in0=qr, in1=c128[:], op=AT.subtract)
                        g.drain()
                        g.tensor_tensor(out=w2q[:, kk * 1024:(kk + 1) * 1024],
                                        in0=qr, in1=srep2[:, sl * 1024:(sl + 1) * 1024], op=AT.mult)
                        g.drain()
                        if kk + 2 < 8:
                            g.dma_start(out=qraw2[:, sl * 1024:(sl + 1) * 1024],
                                        in_=q2[1024 * q + 128 * (kk + 2):1024 * q + 128 * (kk + 3), :]).then_inc(gq2S[sl], 16)
                            g.dma_start(out=srep2[:, sl * 1024:(sl + 1) * 1024],
                                        in_=s2r[1024 * q + 128 * (kk + 2):1024 * q + 128 * (kk + 3), :]).then_inc(gq2S[sl], 16)
                    g.nop().then_inc(gW2, 1)

                def y1_stream(j):
                    q, ft = j // 4, j % 4
                    if j >= 2:
                        g.wait_ge(tF2, 8 * (j - 1))
                    g.dma_start(
                        out=y1s[j % 2].ap().rearrange("p (fc t) -> p fc t", fc=8),
                        in_=y1scr.ap()[1024 * q:1024 * (q + 1), 512 * ft:512 * (ft + 1)]
                            .rearrange("(fc p) t -> p fc t", p=128),
                    ).then_inc(dYS2[j % 2], 16)

                dequant_quarter(0)
                y1_stream(0)
                y1_stream(1)
                for gi2 in range(16):
                    for zj in (range(16 * (gi2 - 1), 16 * (gi2 - 1) + 16) if 1 <= gi2 <= 8 else []):
                        if zj >= 32:
                            g.wait_ge(sZ, zj + 1)
                            zsl = z16[:, (zj % 8) * 2048 + ((zj // 8) % 4) * 512:(zj % 8) * 2048 + ((zj // 8) % 4) * 512 + 512]
                            g.tensor_tensor(out=zsl, in0=zsl,
                                            in1=ztmp[:, (zj % 4) * 512:(zj % 4 + 1) * 512],
                                            op=AT.add)
                            g.drain()
                            g.nop().then_inc(gZ, 1)
                    for jj in (2 * gi2, 2 * gi2 + 1):
                        if 2 <= jj < 16:
                            y1_stream(jj)
                    if gi2 >= 1:
                        j = gi2 - 1
                        for half in range(2):
                            g.wait_ge(sHS, 2 * j + half + 1)
                            g.dma_start(out=snd2_slice2(j, half),
                                        in_=hs16[:, half * 1024:(half + 1) * 1024]).then_inc(gd2S[half], 16)
                    if gi2 in (2, 4, 6):
                        dequant_quarter(gi2 // 2)
                for half in range(2):
                    g.wait_ge(sHS, 2 * 15 + half + 1)
                    g.dma_start(out=snd2_slice2(15, half),
                                in_=hs16[:, half * 1024:(half + 1) * 1024]).then_inc(gd2S[half], 16)
                g.wait_ge(gd2S[0], 16 * 16)
                g.wait_ge(gd2S[1], 16 * 16)
                with allow_ici_all_to_all():
                    g.collective_compute(
                        "AllToAll", AT.bypass, replica_groups=[list(range(NC))],
                        ins=[snd2.ap().opt()], outs=[rcv2.ap().opt()],
                    ).then_inc(cc2, 1)
                g.nop().then_inc(bar, 1)

            blk.sync(b_sync)
            blk.scalar(b_scalar)
            blk.vector(b_vector)
            blk.tensor(b_tensor)
            blk.gpsimd(b_gpsimd)

        _stB.close()

        _stC = contextlib.ExitStack()

        def _sbC(name, shape, dt):
            return _stC.enter_context(nc.sbuf_tensor(name, shape, dt))

        def _psC(name, shape, dt):
            return _stC.enter_context(nc.psum_tensor(name, shape, dt))

        hsC = _sbC("hsC", [128, 2 * 2048], F16)
        xcvC = _sbC("xcvC", [128, 2 * 2048], F16)
        uC = _sbC("uC", [128, 2 * 2048], F16)
        tmp16 = _sbC("tmp16", [128, 2048], F16)
        y16f = _sbC("y16f", [128, 16 * 2048], F16)
        woT = _sbC("woT", [128, 16 * 1024], F16)
        a16t = _sbC("a16t", [128, 2 * 512], F16)
        atm = _sbC("atm", [128, 4 * 1024], F16)
        ztm = _sbC("ztm", [128, 4 * 1024], F16)
        o1 = _sbC("o1", [128, 1024], F32)
        o2 = _sbC("o2", [128, 1024], F32)
        outt = _sbC("outt", [128, 2 * 1024], F32)
        xtC = _sbC("xtC", [128, 2 * D], F32)
        psW0 = _psC("psW0", [128, 512], F32)
        psW1 = _psC("psW1", [128, 512], F32)
        psX0 = _psC("psX0", [128, 512], F16)
        psX1 = _psC("psX1", [128, 512], F16)
        with nc.Block("C") as blkC:
            psW = [psW0, psW1]

            def xtile_c(ft):
                return xtC[:, (ft % 2) * D:(ft % 2 + 1) * D]

            def c_sync(e):
                e.wait_ge(bar, 10)
                for cc in range(16):
                    if cc == 0:
                        e.wait_ge(cc2, 1)
                    if cc >= 2:
                        e.wait_ge(vY, cc - 1)
                    e.dma_start(out=hsC[:, (cc % 2) * 2048:(cc % 2 + 1) * 2048],
                                in_=rcv2.ap()[128 * cc:128 * (cc + 1), :]).then_inc(dCS[cc % 2], 16)
                    e.dma_start(out=xcvC[:, (cc % 2) * 2048:(cc % 2 + 1) * 2048],
                                in_=snd1.ap()[128 * cc:128 * (cc + 1), :]).then_inc(dCS[cc % 2], 16)
                    e.dma_start(out=uC[:, (cc % 2) * 2048:(cc % 2 + 1) * 2048],
                                in_=uscr.ap()[128 * cc:128 * (cc + 1), :]).then_inc(dCS[cc % 2], 16)
                e.nop().then_inc(bar, 1)

            def c_gpsimd(g):
                g.wait_ge(bar, 10)
                g.dma_start(out=woT.ap().rearrange("p (cc o) -> p cc o", cc=16),
                            in_=woutt.rearrange("(cc p) o -> p cc o", p=128)).then_inc(gWo, 16)
                g.dma_start(out=dvec_sb[:], in_=dvec).then_inc(gWo, 16)

                g.dma_start(out=xtile_c(0), in_=xs[128 * 1:128 * 2, :]).then_inc(dxCS[0], 16)
                g.dma_start(out=xtile_c(1), in_=xs[128 * 2:128 * 3, :]).then_inc(dxCS[1], 16)
                for ft in range(16):
                    g.wait_ge(vOut, ft + 1)
                    g.dma_start(out=out[128 * ft:128 * (ft + 1), :],
                                in_=outt[:, (ft % 2) * D:(ft % 2 + 1) * D]).then_inc(dOutS[ft % 2], 16)
                    if ft + 2 < 16:
                        g.dma_start(out=xtile_c(ft + 2),
                                    in_=xs[128 * (ft + 3):128 * (ft + 4), :]).then_inc(dxCS[ft % 2], 16)
                if debug:
                    g.dma_start(out=dbg["dbg_snd1"], in_=snd1.ap()).then_inc(dOut, 16)
                    g.dma_start(out=dbg["dbg_rcv2"], in_=rcv2.ap()).then_inc(dOut, 16)
                    g.dma_start(out=dbg["dbg_y1"], in_=y1scr.ap()).then_inc(dOut, 16)
                    g.dma_start(out=dbg["dbg_u"], in_=uscr.ap()).then_inc(dOut, 16)
                g.wait_ge(dOutS[0], 16 * 8)
                g.wait_ge(dOutS[1], 16 * 8)
                if debug:
                    g.wait_ge(dOut, 16 * 4)

            def c_vector(v):
                v.wait_ge(bar, 10)
                v.wait_ge(sZ, 128)
                v.wait_ge(gZ, 96)
                for oc in range(8):
                    v.tensor_scalar(out=z16[:, oc * 2048:(oc + 1) * 2048],
                                    in0=z16[:, oc * 2048:(oc + 1) * 2048],
                                    scalar1=b2_sb[:, oc:oc + 1], scalar2=None,
                                    op0=AT.add)
                    v.drain()
                    v.engine_nop().then_inc(gZ, 1)
                v.wait_ge(gWo, 32)
                v.drain()
                v.tensor_scalar(out=r1p1[:], in0=r1all[:], scalar1=1.0,
                                scalar2=None, op0=AT.add)
                v.drain()
                for cc in range(16):
                    v.wait_ge(dCS[cc % 2], 48 * (cc // 2 + 1))
                    v.tensor_tensor(out=tmp16[:],
                                    in0=uC[:, (cc % 2) * 2048:(cc % 2 + 1) * 2048],
                                    in1=hsC[:, (cc % 2) * 2048:(cc % 2 + 1) * 2048],
                                    op=AT.mult)
                    v.drain()
                    v.scalar_tensor_tensor(out=y16f[:, cc * 2048:(cc + 1) * 2048],
                                           in0=xcvC[:, (cc % 2) * 2048:(cc % 2 + 1) * 2048],
                                           scalar=dvec_sb[:, cc:cc + 1],
                                           in1=tmp16[:], op0=AT.mult,
                                           op1=AT.add).then_inc(vY, 1)
                    v.drain()
                for ft in range(16):
                    tt, ii = ft // 4, ft % 4
                    v.wait_ge(dxCS[ft % 2], 16 * (ft // 2 + 1))
                    v.wait_ge(sTr, 2 * (8 * tt + 8))
                    if ft >= 2:
                        v.wait_ge(dOutS[ft % 2], 16 * (ft // 2))
                    v.tensor_scalar(out=o1[:], in0=xtile_c(ft),
                                    scalar1=r1p1[:, ft + 1:ft + 2],
                                    scalar2=nmr1all[:, ft + 1:ft + 2],
                                    op0=AT.mult, op1=AT.add)
                    v.scalar_tensor_tensor(out=o2[:], in0=ztm[:, ii * 1024:(ii + 1) * 1024],
                                           scalar=0.5, in1=atm[:, ii * 1024:(ii + 1) * 1024],
                                           op0=AT.mult, op1=AT.add)
                    v.drain()
                    v.tensor_tensor(out=outt[:, (ft % 2) * D:(ft % 2 + 1) * D],
                                    in0=o1[:], in1=o2[:], op=AT.add).then_inc(vOut, 1)
                    v.drain()

            def c_tensor(t):
                t.wait_ge(bar, 10)
                t.wait_ge(gWo, 32)

                def wout_group(wi):
                    tt, oc = wi // 8, wi % 8
                    if wi >= 2:
                        t.wait_ge(sW, wi - 1)
                    for cc in range(16):
                        if wi == 0:
                            t.wait_ge(vY, cc + 1)
                        ins = t.matmul(out=psW[wi % 2][:, :],
                                       lhsT=woT[:, cc * 1024 + oc * 128:cc * 1024 + (oc + 1) * 128],
                                       rhs=y16f[:, cc * 2048 + tt * 512:cc * 2048 + (tt + 1) * 512],
                                       start=(cc == 0), stop=(cc == 15))
                    ins.then_inc(tW, 1)

                def transpose_pair(wi):
                    tt, oc = wi // 8, wi % 8
                    t.wait_ge(sW, wi + 1)
                    if wi >= 1:
                        t.wait_ge(sTr, 2 * wi - 1)
                    if tt >= 1 and oc == 0:
                        t.wait_ge(vOut, 4 * tt)
                    for ii in range(4):
                        t.transpose(out=psX0[:, ii * 128:(ii + 1) * 128],
                                    in_=a16t[:, (wi % 2) * 512 + ii * 128:(wi % 2) * 512 + (ii + 1) * 128],
                                    identity=eye_sb[:])
                    t.drain()
                    t.nop().then_inc(tTr, 1)
                    t.wait_ge(gZ, 96 + oc + 1)
                    for ii in range(4):
                        t.transpose(out=psX1[:, ii * 128:(ii + 1) * 128],
                                    in_=z16[:, oc * 2048 + tt * 512 + ii * 128:oc * 2048 + tt * 512 + (ii + 1) * 128],
                                    identity=eye_sb[:])
                    t.drain()
                    t.nop().then_inc(tTr, 1)

                for wi in range(32):
                    wout_group(wi)
                    if wi >= 1:
                        transpose_pair(wi - 1)
                transpose_pair(31)

            def c_scalar(s):
                s.wait_ge(bar, 10)
                for wi in range(32):
                    tt, oc = wi // 8, wi % 8
                    s.wait_ge(tW, wi + 1)
                    if wi >= 2:
                        s.wait_ge(tTr, 2 * (wi - 2) + 1)
                    s.copy(out=a16t[:, (wi % 2) * 512:(wi % 2 + 1) * 512],
                           in_=psW[wi % 2][:, :]).then_inc(sW, 1)
                    s.wait_ge(tTr, 2 * wi + 1)
                    for ii in range(4):
                        s.copy(out=atm[:, ii * 1024 + oc * 128:ii * 1024 + (oc + 1) * 128],
                               in_=psX0[:, ii * 128:(ii + 1) * 128])
                    s.drain()
                    s.nop().then_inc(sTr, 1)
                    s.wait_ge(tTr, 2 * wi + 2)
                    for ii in range(4):
                        s.copy(out=ztm[:, ii * 1024 + oc * 128:ii * 1024 + (oc + 1) * 128],
                               in_=psX1[:, ii * 128:(ii + 1) * 128])
                    s.drain()
                    s.nop().then_inc(sTr, 1)


            blkC.sync(c_sync)
            blkC.gpsimd(c_gpsimd)
            blkC.vector(c_vector)
            blkC.tensor(c_tensor)
            blkC.scalar(c_scalar)
        _stC.close()
        _stZ.close()

    return nc


def kernel(**inputs):
    x = np.asarray(inputs["L0_attn_x"])[0]
    W_in = np.asarray(inputs["L0_attn_W_in"])
    conv_w = np.asarray(inputs["L0_attn_conv_w"])
    conv_b = np.asarray(inputs["L0_attn_conv_b"])
    A_log = np.asarray(inputs["L0_attn_A_log"])
    Dv = np.asarray(inputs["L0_attn_D"])
    W_out = np.asarray(inputs["L0_attn_W_out"])
    g = np.asarray(inputs["L0_ffn_ln_gamma"])
    b = np.asarray(inputs["L0_ffn_ln_beta"])
    fc1_q = np.asarray(inputs["L0_ffn_fc1_q"])
    fc1_s = np.asarray(inputs["L0_ffn_fc1_scales"])
    fc1_b = np.asarray(inputs["L0_ffn_fc1_bias"])
    fc2_q = np.asarray(inputs["L0_ffn_fc2_q"])
    fc2_s = np.asarray(inputs["L0_ffn_fc2_scales"])
    fc2_b = np.asarray(inputs["L0_ffn_fc2_bias"])

    xpad = np.concatenate([np.zeros((128, D), np.float32), x], axis=0)
    wint = np.ascontiguousarray(W_in.T)
    woutt = np.ascontiguousarray(W_out.T)
    convw = np.ascontiguousarray(conv_w[:, 0, :].reshape(16, 128, 4).transpose(1, 0, 2).reshape(128, 64))
    convb = np.ascontiguousarray(conv_b.reshape(16, 128).T)
    dvec = np.ascontiguousarray(Dv.reshape(16, 128).T)
    gam = np.ascontiguousarray(np.broadcast_to(g, (128, D)))
    bet = np.ascontiguousarray(np.broadcast_to(b, (128, D)))
    s1r = np.ascontiguousarray(np.repeat(fc1_s, 128, axis=0))
    b1m = np.ascontiguousarray(fc1_b.reshape(32, 128).T)
    s2r = np.ascontiguousarray(np.repeat(fc2_s, 128, axis=0))
    b2m = np.ascontiguousarray(fc2_b.reshape(8, 128).T)
    eyem = np.eye(128, dtype=np.float16)
    zzm = np.zeros((128, 1), np.float32)
    cstm = np.ascontiguousarray(
        np.broadcast_to(np.array([EPS, 1.0], np.float32), (128, 2)))
    alog_r = A_log.reshape(16, 128, NST)

    nc = build_nc(debug=False)
    in_maps = []
    for c in range(NC):
        al = np.ascontiguousarray(
            np.concatenate([alog_r[2 * c], alog_r[2 * c + 1]], axis=1))
        in_maps.append({
            "xs": np.ascontiguousarray(xpad[TLOC * c:TLOC * c + TEXT]),
            "wint": wint, "woutt": woutt, "convw": convw, "convb": convb,
            "alog": al, "dvec": dvec, "gam": gam, "bet": bet,
            "q1": fc1_q, "s1r": s1r, "b1": b1m,
            "q2": fc2_q, "s2r": s2r, "b2": b2m,
            "eye": eyem, "zz": zzm, "cst": cstm,
        })
    res = run_bass_kernel_spmd(nc, in_maps, core_ids=list(range(NC)))
    full = np.concatenate([np.asarray(res.results[c]["out"]) for c in range(NC)], axis=0)
    return full.reshape(1, 16384, D).astype(np.float32)


# version 21 (final)
# speedup vs baseline: 1.0228x; 1.0228x over previous
import contextlib
import numpy as np
import ml_dtypes
import concourse.bass as bass
import concourse.mybir as mybir
from concourse.bass_utils import run_bass_kernel_spmd
from concourse.aot_env import allow_ici_all_to_all

F32 = mybir.dt.float32
F16 = mybir.dt.float16
I32 = mybir.dt.int32
AT = mybir.AluOpType
AF = mybir.ActivationFunctionType

NC = 8
TLOC = 2048
TEXT = 2176
NTILE = 17
D = 1024
INNER = 2048
NST = 16
FF = 4096
EPS = 1e-5
SSM_NT = [512, 512, 512, 512, 128]

def _grp_tiles(p):
    return 5 if p % 2 == 0 else 4

CUMT = [0]
for _p in range(32):
    CUMT.append(CUMT[-1] + _grp_tiles(_p))
def _sp1_ssm(k):
    return CUMT[2 * k] + 5
def _sp1_res(k):
    return CUMT[2 * k + 2]

SA_A_TOTAL = 1 + NTILE * 4


def build_nc(debug=False):
    nc = bass.Bass("TRN2", target_bir_lowering=False)

    xs = nc.dram_tensor("xs", [TEXT, D], F32, kind="ExternalInput").ap()
    wint = nc.dram_tensor("wint", [D, 2 * INNER], F32, kind="ExternalInput").ap()
    woutt = nc.dram_tensor("woutt", [INNER, D], F32, kind="ExternalInput").ap()
    convw = nc.dram_tensor("convw", [128, 64], F32, kind="ExternalInput").ap()
    convb = nc.dram_tensor("convb", [128, 16], F32, kind="ExternalInput").ap()
    alog = nc.dram_tensor("alog", [128, 2 * NST], F32, kind="ExternalInput").ap()
    dvec = nc.dram_tensor("dvec", [128, 16], F32, kind="ExternalInput").ap()
    gam = nc.dram_tensor("gam", [128, D], F32, kind="ExternalInput").ap()
    bet = nc.dram_tensor("bet", [128, D], F32, kind="ExternalInput").ap()
    q1 = nc.dram_tensor("q1", [D, FF], I32, kind="ExternalInput").ap()
    s1r = nc.dram_tensor("s1r", [D, FF], F32, kind="ExternalInput").ap()
    b1 = nc.dram_tensor("b1", [128, 32], F32, kind="ExternalInput").ap()
    q2 = nc.dram_tensor("q2", [FF, D], I32, kind="ExternalInput").ap()
    s2r = nc.dram_tensor("s2r", [FF, D], F32, kind="ExternalInput").ap()
    b2 = nc.dram_tensor("b2", [128, 8], F32, kind="ExternalInput").ap()
    eye = nc.dram_tensor("eye", [128, 128], F16, kind="ExternalInput").ap()
    zz = nc.dram_tensor("zz", [128, 1], F32, kind="ExternalInput").ap()
    cst = nc.dram_tensor("cst", [128, 2], F32, kind="ExternalInput").ap()
    out = nc.dram_tensor("out", [TLOC, D], F32, kind="ExternalOutput").ap()
    dbg = {}
    if debug:
        for nm, shp, dt in [("dbg_snd1", [INNER, TLOC], F16),
                            ("dbg_rcv2", [INNER, TLOC], F16),
                            ("dbg_y1", [FF, TLOC], F16),
                            ("dbg_u", [INNER, TLOC], F16)]:
            dbg[nm] = nc.dram_tensor(nm, shp, dt, kind="ExternalOutput").ap()

    snd1 = nc.dram_tensor("snd1", [INNER, TLOC], F16)
    rcv1 = nc.dram_tensor("rcv1", [INNER, TLOC], F16)
    snd2 = nc.dram_tensor("snd2", [INNER, TLOC], F16)
    rcv2 = nc.dram_tensor("rcv2", [INNER, TLOC], F16)
    uscr = nc.dram_tensor("uscr", [INNER, TLOC], F16)
    y1scr = nc.dram_tensor("y1scr", [FF, TLOC], F16)
    yscr = nc.dram_tensor("yscr", [INNER, TLOC], F16)

    with contextlib.ExitStack() as _st:
        def _sb(name, shape, dt):
            return _st.enter_context(nc.sbuf_tensor(name, shape, dt))

        def _sem(name):
            return _st.enter_context(nc.semaphore(name))

        eye_sb = _sb("eye_sb", [128, 128], F16)
        zz_sb = _sb("zz_sb", [128, 1], F32)
        convw_sb = _sb("convw_sb", [128, 64], F32)
        convb_sb = _sb("convb_sb", [128, 16], F32)
        alog_sb = _sb("alog_sb", [128, 2 * NST], F32)
        a32 = _sb("a32", [128, 2 * NST], F32)
        tmpA = _sb("tmpA", [128, 2 * NST], F32)
        dvec_sb = _sb("dvec_sb", [128, 16], F32)
        b1_sb = _sb("b1_sb", [128, 32], F32)
        b2_sb = _sb("b2_sb", [128, 8], F32)
        r1all = _sb("r1all", [128, NTILE], F32)
        nmr1all = _sb("nmr1all", [128, NTILE], F32)
        r2all = _sb("r2all", [128, NTILE], F32)
        nmr2all = _sb("nmr2all", [128, NTILE], F32)
        st1all = _sb("st1all", [128, 2 * NTILE], F32)
        st2all = _sb("st2all", [128, 2 * NTILE], F32)
        sq_all = _sb("sq_all", [128, NTILE], F32)
        tmp_all = _sb("tmp_all", [128, NTILE], F32)
        stat_scr = _sb("stat_scr", [128, 12], F32)
        r1p1 = _sb("r1p1", [128, NTILE], F32)
        c128 = _sb("c128", [128, 1024], F32)
        cst_sb = _sb("cst_sb", [128, 2], F32)
        nc.const_aps.register(F32, EPS, cst_sb[:, 0:1])
        nc.const_aps.register(F32, 1.0, cst_sb[:, 1:2])
        (dx, gcst, vA, sA, tA, eA, gwb, tP1, sP1, sXCV, gC, gd, gq, gW1, tF1,
         sF1, y1d, cc1, bar, dB, sB, sAD, vS, tSG, sHS, gd2, dY, gq2, gW2,
         tF2, sZ, gZ, cc2, dC, vY, gdY, dYs, gWo, tW, sW, tTr, sTr, vOut,
         dOut, vCV, dxC, sA2, gGB) = [_sem(n) for n in (
            "dx", "gcst", "vA", "sA", "tA", "eA", "gwb", "tP1", "sP1", "sXCV",
            "gC", "gd", "gq", "gW1", "tF1", "sF1", "y1d", "cc1", "bar", "dB",
            "sB", "sAD", "vS", "tSG", "sHS", "gd2", "dY", "gq2", "gW2", "tF2",
            "sZ", "gZ", "cc2", "dC", "vY", "gdY", "dYs", "gWo", "tW", "sW",
            "tTr", "sTr", "vOut", "dOut", "vCV", "dxC", "sA2", "gGB")]
        gdS = [_sem("gdS0"), _sem("gdS1")]
        y1S = [_sem("y1S0"), _sem("y1S1"), _sem("y1S2"), _sem("y1S3")]
        gd2S = [_sem("gd2S0"), _sem("gd2S1")]
        dBS = [_sem("dBS0"), _sem("dBS1"), _sem("dBS2"), _sem("dBS3")]
        dYS2 = [_sem("dYS20"), _sem("dYS21")]
        dCS = [_sem("dCS0"), _sem("dCS1")]
        dYsS = [_sem("dYsS0"), _sem("dYsS1")]
        dxCS = [_sem("dxCS0"), _sem("dxCS1")]
        gdYS = [_sem("gdYS0"), _sem("gdYS1")]
        gqS = [_sem("gqS0"), _sem("gqS1")]
        sCR = _sem("sCR")
        gq2S = [_sem("gq2S0"), _sem("gq2S1")]
        dOutS = [_sem("dOutS0"), _sem("dOutS1")]

        _stA = contextlib.ExitStack()

        def _sbA(name, shape, dt):
            return _stA.enter_context(nc.sbuf_tensor(name, shape, dt))

        def _psA(name, shape, dt):
            return _stA.enter_context(nc.psum_tensor(name, shape, dt))

        xt0 = _sbA("xt0", [128, D], F32)
        xt1 = _sbA("xt1", [128, D], F32)
        xn32 = _sbA("xn32", [128, D], F32)
        xn16r = _sbA("xn16r", [128, 8 * D], F16)
        hf16 = _sbA("hf16", [128, 2 * D], F16)
        gam16 = _sbA("gam16", [128, D], F16)
        bet16 = _sbA("bet16", [128, D], F16)
        xnT = _sbA("xnT", [128, 8 * TEXT], F16)
        hfT = _sbA("hfT", [128, 8 * TEXT], F16)
        wb0 = _sbA("wb0", [128, 8 * 128], F16)
        wb1 = _sbA("wb1", [128, 8 * 128], F16)
        wb2 = _sbA("wb2", [128, 8 * 128], F16)
        wb3 = _sbA("wb3", [128, 8 * 128], F16)
        xssm0 = _sbA("xssm0", [128, TEXT], F16)
        xssm1 = _sbA("xssm1", [128, TEXT], F16)
        cacc0 = _sbA("cacc0", [128, TLOC], F16)
        cacc1 = _sbA("cacc1", [128, TLOC], F16)
        xcv16 = _sbA("xcv16", [128, 2 * TLOC], F16)
        sres16 = _sbA("sres16", [128, 2 * TLOC], F16)
        u16t = _sbA("u16t", [128, TLOC], F16)
        qraw = _sbA("qraw", [128, 2 * 1024], F32)
        srep = _sbA("srep", [128, 2 * 1024], F32)
        w1q = _sbA("w1q", [128, 2 * 8 * 1024], F16)
        y1t = _sbA("y1t", [128, 4 * 512], F16)
        psA0 = _psA("psA0", [128, 512], F32)
        psA1 = _psA("psA1", [128, 512], F32)
        psC0 = _psA("psC0", [128, 512], F32)
        psC1 = _psA("psC1", [128, 512], F32)
        psC2 = _psA("psC2", [128, 512], F32)
        psC3 = _psA("psC3", [128, 512], F32)
        psT0 = _psA("psT0", [128, 1024], F16)
        psT1 = _psA("psT1", [128, 1024], F16)
        with nc.Block("A") as blk:
            psA = [psA0, psA1]
            psC = [psC0, psC1, psC2, psC3]
            xtile = [xt0, xt1]
            xssm = [xssm0, xssm1]
            cacc = [cacc0, cacc1]
            wb = [wb0, wb1, wb2, wb3]

            def a_sync(e):
                e.dma_start(out=alog_sb[:], in_=alog).then_inc(gcst, 16)
                e.dma_start(out=cst_sb[:], in_=cst).then_inc(gcst, 16)
                e.dma_start(out=eye_sb[:], in_=eye).then_inc(gcst, 16)
                e.dma_start(out=zz_sb[:], in_=zz).then_inc(gcst, 16)
                e.dma_start(out=convw_sb[:], in_=convw).then_inc(gcst, 16)
                e.dma_start(out=convb_sb[:], in_=convb).then_inc(gcst, 16)
                e.dma_start(out=b1_sb[:], in_=b1).then_inc(gcst, 16)
                e.dma_start(out=b2_sb[:], in_=b2).then_inc(gcst, 16)
                for tt in range(NTILE):
                    if tt >= 2:
                        e.wait_ge(vA, tt - 1)
                    e.dma_start(out=xtile[tt % 2][:],
                                in_=xs[128 * tt:128 * (tt + 1), :]).then_inc(dx, 16)
                for tt in range(NTILE):
                    e.wait_ge(vA, 18 if tt < 2 else 18 + tt)
                    e.dma_start(out=xtile[tt % 2][:],
                                in_=xs[128 * tt:128 * (tt + 1), :]).then_inc(dx, 16)
                for tt in range(NTILE):
                    e.wait_ge(vA, 38 if tt < 2 else 38 + tt - 1)
                    e.dma_start(out=xtile[tt % 2][:],
                                in_=xs[128 * tt:128 * (tt + 1), :]).then_inc(dx, 16)
                for j in range(128):
                    q, ft, fc = j // 32, (j // 8) % 4, j % 8
                    e.wait_ge(sF1, j + 1)
                    e.dma_start(out=y1scr.ap()[1024 * q + 128 * fc:1024 * q + 128 * (fc + 1),
                                               512 * ft:512 * (ft + 1)],
                                in_=y1t[:, (j % 4) * 512:(j % 4 + 1) * 512]).then_inc(y1S[j % 4], 16)
                for _sl in range(4):
                    e.wait_ge(y1S[_sl], 16 * 32)
                e.nop().then_inc(bar, 1)

            def a_vector(v):
                v.wait_ge(sA2, 1)
                v.tensor_scalar(out=a32[:], in0=tmpA[:], scalar1=-1.0, scalar2=None,
                                op0=AT.mult)
                v.drain()
                for tt in range(NTILE):
                    v.wait_ge(dx, 16 * min(tt + 2, NTILE))
                    xt = xtile[tt % 2]
                    v.bn_stats(out=stat_scr[:, 0:6], in_=xt[:, 0:512])
                    v.bn_stats(out=stat_scr[:, 6:12], in_=xt[:, 512:1024])
                    v.drain()
                    v.bn_aggr(out=st1all[:, 2 * tt:2 * tt + 2], in_=stat_scr[:, 0:12])
                    v.drain()
                    v.engine_nop().then_inc(vA, 1)
                v.drain()
                v.engine_nop().then_inc(vA, 1)
                v.wait_ge(sA, 1)
                v.reciprocal(out=r1all[:], in_=sq_all[:])
                v.drain()
                v.tensor_tensor(out=tmp_all[:], in0=st1all.ap().rearrange("p (t two) -> p t two", two=2)[:, :, 0],
                                in1=r1all[:], op=AT.mult)
                v.drain()
                v.tensor_scalar(out=nmr1all[:], in0=tmp_all[:], scalar1=-1.0,
                                scalar2=None, op0=AT.mult)
                v.drain()
                v.engine_nop().then_inc(vA, 1)
                for tt in range(NTILE):
                    v.wait_ge(dx, 16 * (NTILE + min(tt + 2, NTILE)))
                    if tt >= 8:
                        v.wait_ge(tA, tt - 7)
                    xt = xtile[tt % 2]
                    v.tensor_scalar(out=xn32[:], in0=xt[:], scalar1=r1all[:, tt:tt + 1],
                                    scalar2=nmr1all[:, tt:tt + 1], op0=AT.mult, op1=AT.add)
                    v.drain()
                    v.tensor_copy(out=xn16r[:, (tt % 8) * D:(tt % 8 + 1) * D], in_=xn32[:])
                    v.bn_stats(out=stat_scr[:, 0:6], in_=xn32[:, 0:512])
                    v.bn_stats(out=stat_scr[:, 6:12], in_=xn32[:, 512:1024])
                    v.drain()
                    v.bn_aggr(out=st2all[:, 2 * tt:2 * tt + 2], in_=stat_scr[:, 0:12])
                    v.drain()
                    v.engine_nop().then_inc(vA, 1)
                v.drain()
                v.engine_nop().then_inc(vA, 1)
                v.wait_ge(sA, 2)
                v.reciprocal(out=r2all[:], in_=sq_all[:])
                v.drain()
                v.tensor_tensor(out=tmp_all[:], in0=st2all.ap().rearrange("p (t two) -> p t two", two=2)[:, :, 0],
                                in1=r2all[:], op=AT.mult)
                v.drain()
                v.tensor_scalar(out=nmr2all[:], in0=tmp_all[:], scalar1=-1.0,
                                scalar2=None, op0=AT.mult)
                v.drain()
                v.engine_nop().then_inc(vA, 1)
                v.wait_ge(gcst, 16 * 8)
                v.wait_ge(gGB, 32)
                for tt in range(NTILE):
                    v.wait_ge(dx, 16 * (2 * NTILE + min(tt + 2, NTILE)))
                    if tt >= 2:
                        v.wait_ge(tA, NTILE + tt - 1)
                    v.tensor_scalar(out=xn32[:], in0=xtile[tt % 2][:],
                                    scalar1=r1all[:, tt:tt + 1],
                                    scalar2=nmr1all[:, tt:tt + 1], op0=AT.mult, op1=AT.add)
                    v.drain()
                    hfp = hf16[:, (tt % 2) * D:(tt % 2 + 1) * D]
                    v.tensor_scalar(out=hfp, in0=xn32[:],
                                    scalar1=r2all[:, tt:tt + 1],
                                    scalar2=nmr2all[:, tt:tt + 1], op0=AT.mult, op1=AT.add)
                    v.drain()
                    v.tensor_tensor(out=hfp, in0=hfp, in1=gam16[:], op=AT.mult)
                    v.drain()
                    v.tensor_tensor(out=hfp, in0=hfp, in1=bet16[:], op=AT.add)
                    v.drain()
                    v.engine_nop().then_inc(vA, 1)
                for k in range(16):
                    v.wait_ge(sP1, _sp1_ssm(k))
                    if k >= 2:
                        v.wait_ge(sXCV, k - 1)
                    xsk = xssm[k % 2]
                    v.tensor_scalar(out=cacc[k % 2][:], in0=xsk[:, 125:125 + TLOC],
                                    scalar1=convw_sb[:, 4 * k:4 * k + 1],
                                    scalar2=None, op0=AT.mult)
                    for tau in range(1, 4):
                        v.drain()
                        v.scalar_tensor_tensor(out=cacc[k % 2][:],
                                               in0=xsk[:, 125 + tau:125 + tau + TLOC],
                                               scalar=convw_sb[:, 4 * k + tau:4 * k + tau + 1],
                                               in1=cacc[k % 2][:], op0=AT.mult, op1=AT.add)
                    v.drain()
                    v.engine_nop().then_inc(vCV, 1)
                v.engine_nop().then_inc(bar, 1)

            def a_scalar(s):
                s.wait_ge(gcst, 16 * 8)
                s.activation(out=tmpA[:], in_=alog_sb[:], func=AF.Exp).then_inc(sA2, 1)
                s.wait_ge(vA, 18)
                s.activation(out=sq_all[:],
                             in_=st1all.ap().rearrange("p (t two) -> p t two", two=2)[:, :, 1],
                             func=AF.Sqrt, bias=EPS).then_inc(sA, 1)
                for tt in range(NTILE):
                    s.wait_ge(tA, tt + 1)
                    for dd in range(8):
                        s.copy(out=xnT[:, dd * TEXT + 128 * tt:dd * TEXT + 128 * (tt + 1)],
                               in_=psT0[:, 128 * dd:128 * (dd + 1)])
                    s.drain()
                    s.nop().then_inc(eA, 1)
                s.wait_ge(vA, 37)
                s.activation(out=sq_all[:],
                             in_=st2all.ap().rearrange("p (t two) -> p t two", two=2)[:, :, 1],
                             func=AF.Sqrt, bias=EPS).then_inc(sA, 1)
                for tt in range(NTILE):
                    s.wait_ge(tA, NTILE + tt + 1)
                    for dd in range(8):
                        s.copy(out=hfT[:, dd * TEXT + 128 * tt:dd * TEXT + 128 * (tt + 1)],
                               in_=psT1[:, 128 * dd:128 * (dd + 1)])
                    s.drain()
                    s.nop().then_inc(eA, 1)
                for k in range(16):
                    for nt in range(5):
                        s.wait_ge(tP1, CUMT[2 * k] + nt + 1)
                        if k >= 2 and nt == 0:
                            s.wait_ge(vCV, k - 1)
                        n = SSM_NT[nt]
                        s.copy(out=xssm[k % 2][:, 512 * nt:512 * nt + n],
                               in_=psA[(CUMT[2 * k] + nt) % 2][:, 0:n]).then_inc(sP1, 1)
                    s.wait_ge(vCV, k + 1)
                    if k >= 2:
                        s.wait_ge(gdS[k % 2], 32 * (k // 2))
                    s.activation(out=xcv16[:, (k % 2) * TLOC:(k % 2 + 1) * TLOC],
                                 in_=cacc[k % 2][:], func=AF.Silu,
                                 bias=convb_sb[:, k:k + 1]).then_inc(sXCV, 1)
                    for nt in range(4):
                        s.wait_ge(tP1, CUMT[2 * k + 1] + nt + 1)
                        if k >= 1 and nt == 0:
                            s.wait_ge(gC, k)
                        s.activation(out=sres16[:, (k % 2) * TLOC + 512 * nt:(k % 2) * TLOC + 512 * (nt + 1)],
                                     in_=psA[(CUMT[2 * k + 1] + nt) % 2][:, :],
                                     func=AF.Silu).then_inc(sP1, 1)
                for j in range(128):
                    q, fc = j // 32, j % 8
                    s.wait_ge(tF1, j + 1)
                    if j >= 4:
                        s.wait_ge(y1S[j % 4], 16 * (j // 4))
                    fcg = q * 8 + fc
                    s.activation(out=y1t[:, (j % 4) * 512:(j % 4 + 1) * 512],
                                 in_=psC[j % 4][:, :], func=AF.Silu,
                                 bias=b1_sb[:, fcg:fcg + 1]).then_inc(sF1, 1)
                s.nop().then_inc(bar, 1)

            def a_tensor(t):
                t.wait_ge(gcst, 16 * 8)
                for tt in range(NTILE):
                    t.wait_ge(vA, 19 + tt + 1)
                    if tt >= 1:
                        t.wait_ge(eA, tt)
                    for dd in range(8):
                        t.transpose(out=psT0[:, 128 * dd:128 * (dd + 1)],
                                    in_=xn16r[:, (tt % 8) * D + 128 * dd:(tt % 8) * D + 128 * (dd + 1)],
                                    identity=eye_sb[:])
                    t.drain()
                    t.nop().then_inc(tA, 1)
                for tt in range(NTILE):
                    t.wait_ge(vA, 38 + tt + 1)
                    if tt >= 1:
                        t.wait_ge(eA, NTILE + tt)
                    for dd in range(8):
                        t.transpose(out=psT1[:, 128 * dd:128 * (dd + 1)],
                                    in_=hf16[:, (tt % 2) * D + 128 * dd:(tt % 2) * D + 128 * (dd + 1)],
                                    identity=eye_sb[:])
                    t.drain()
                    t.nop().then_inc(tA, 1)
                t.wait_ge(eA, NTILE)
                for p in range(32):
                    is_ssm = (p % 2 == 0)
                    t.wait_ge(gwb, 16 * min(2 * (p // 2) + 4, 32))
                    ntl = SSM_NT if is_ssm else [512, 512, 512, 512]
                    for nt, n in enumerate(ntl):
                        gidx = CUMT[p] + nt + 1
                        if gidx > 2:
                            t.wait_ge(sP1, gidx - 2)
                        col0 = (nt * 512) if is_ssm else (128 + nt * 512)
                        for dd in range(8):
                            ins = t.matmul(out=psA[(CUMT[p] + nt) % 2][:, 0:n],
                                           lhsT=wb[p % 4][:, 128 * dd:128 * (dd + 1)],
                                           rhs=xnT[:, dd * TEXT + col0:dd * TEXT + col0 + n],
                                           start=(dd == 0), stop=(dd == 7))
                        ins.then_inc(tP1, 1)
                t.wait_ge(eA, 2 * NTILE)
                for j in range(128):
                    q, ft, fc = j // 32, (j // 8) % 4, j % 8
                    if j % 32 == 0:
                        t.wait_ge(gW1, q + 1)
                    if j >= 4:
                        t.wait_ge(sF1, j - 3)
                    for kk in range(8):
                        ins = t.matmul(out=psC[j % 4][:, :],
                                       lhsT=w1q[:, (q % 2) * 8192 + kk * 1024 + fc * 128:(q % 2) * 8192 + kk * 1024 + (fc + 1) * 128],
                                       rhs=hfT[:, kk * TEXT + 128 + ft * 512:kk * TEXT + 128 + (ft + 1) * 512],
                                       start=(kk == 0), stop=(kk == 7))
                    ins.then_inc(tF1, 1)
                t.nop().then_inc(bar, 1)

            def a_gpsimd(g):
                g.memset(c128[:], 128.0)
                g.drain()
                g.dma_start(out=gam16[:], in_=gam).then_inc(gGB, 16)
                g.dma_start(out=bet16[:], in_=bet).then_inc(gGB, 16)

                def dequant_fc1(q):
                    if q >= 2:
                        g.wait_ge(tF1, 32 * (q - 1))
                    for kk in range(8):
                        sl = kk % 2
                        if kk == 0:
                            for pre in range(2):
                                g.dma_start(out=qraw[:, pre * 1024:(pre + 1) * 1024],
                                            in_=q1[128 * pre:128 * (pre + 1), 1024 * q:1024 * (q + 1)]).then_inc(gqS[pre], 16)
                                g.dma_start(out=srep[:, pre * 1024:(pre + 1) * 1024],
                                            in_=s1r[128 * pre:128 * (pre + 1), 1024 * q:1024 * (q + 1)]).then_inc(gqS[pre], 16)
                        g.wait_ge(gqS[sl], 32 * (q * 4 + kk // 2 + 1))
                        qr = qraw[:, sl * 1024:(sl + 1) * 1024]
                        g.tensor_tensor(out=qr, in0=qr, in1=c128[:], op=AT.subtract)
                        g.drain()
                        g.tensor_tensor(out=w1q[:, (q % 2) * 8192 + kk * 1024:(q % 2) * 8192 + (kk + 1) * 1024],
                                        in0=qr, in1=srep[:, sl * 1024:(sl + 1) * 1024], op=AT.mult)
                        g.drain()
                        if kk + 2 < 8:
                            g.dma_start(out=qraw[:, sl * 1024:(sl + 1) * 1024],
                                        in_=q1[128 * (kk + 2):128 * (kk + 3), 1024 * q:1024 * (q + 1)]).then_inc(gqS[sl], 16)
                            g.dma_start(out=srep[:, sl * 1024:(sl + 1) * 1024],
                                        in_=s1r[128 * (kk + 2):128 * (kk + 3), 1024 * q:1024 * (q + 1)]).then_inc(gqS[sl], 16)
                    g.nop().then_inc(gW1, 1)

                dequant_fc1(0)
                dequant_fc1(1)
                def wload(p):
                    k = p // 2
                    col0 = 128 * k if p % 2 == 0 else INNER + 128 * k
                    if p >= 4:
                        g.wait_ge(tP1, CUMT[p - 3])
                    g.dma_start(
                        out=wb[p % 4].ap().rearrange("p (dc i) -> p dc i", dc=8),
                        in_=wint[:, col0:col0 + 128].rearrange("(dc p) i -> p dc i", p=128),
                    ).then_inc(gwb, 16)

                for p in range(4):
                    wload(p)
                for k in range(16):
                    if 2 * k + 4 < 32:
                        wload(2 * k + 4)
                    if 2 * k + 5 < 32:
                        wload(2 * k + 5)
                    g.wait_ge(sXCV, k + 1)
                    g.wait_ge(sP1, _sp1_res(k))
                    if k >= 1:
                        g.wait_ge(gdS[(k - 1) % 2], 32 * ((k - 1) // 2 + 1))
                    g.tensor_tensor(out=u16t[:],
                                    in0=xcv16[:, (k % 2) * TLOC:(k % 2 + 1) * TLOC],
                                    in1=sres16[:, (k % 2) * TLOC:(k % 2 + 1) * TLOC],
                                    op=AT.mult)
                    g.drain()
                    g.nop().then_inc(gC, 1)
                    g.dma_start(out=snd1.ap()[128 * k:128 * (k + 1), :],
                                in_=xcv16[:, (k % 2) * TLOC:(k % 2 + 1) * TLOC]).then_inc(gdS[k % 2], 16)
                    g.dma_start(out=uscr.ap()[128 * k:128 * (k + 1), :],
                                in_=u16t[:]).then_inc(gdS[k % 2], 16)
                dequant_fc1(2)
                dequant_fc1(3)
                g.wait_ge(gdS[0], 32 * 8)
                g.wait_ge(gdS[1], 32 * 8)
                with allow_ici_all_to_all():
                    g.collective_compute(
                        "AllToAll", AT.bypass, replica_groups=[list(range(NC))],
                        ins=[snd1.ap().opt()], outs=[rcv1.ap().opt()],
                    ).then_inc(cc1, 1)
                g.nop().then_inc(bar, 1)

            blk.sync(a_sync)
            blk.vector(a_vector)
            blk.scalar(a_scalar)
            blk.tensor(a_tensor)
            blk.gpsimd(a_gpsimd)

        _stA.close()

        _stZ = contextlib.ExitStack()
        z16 = _stZ.enter_context(nc.sbuf_tensor("z16", [128, 8 * 2048], F16))

        _stB = contextlib.ExitStack()

        def _sbB(name, shape, dt):
            return _stB.enter_context(nc.sbuf_tensor(name, shape, dt))

        def _psB(name, shape, dt):
            return _stB.enter_context(nc.psum_tensor(name, shape, dt))

        xcvs = _sbB("xcvs", [128, 4 * 2048], F16)
        e32 = _sbB("e32", [128, 2048], F32)
        delta32 = _sbB("delta32", [128, 2048], F32)
        bx16 = _sbB("bx16", [128, 4 * 2048], F16)
        ad16 = _sbB("ad16", [128, 8 * 2048], F16)
        hring = _sbB("hring", [128, 8 * 2048], F16)
        carry = _sbB("carry", [128, 32], F16)
        hs16 = _sbB("hs16", [128, 2 * 1024], F16)
        ztmp = _sbB("ztmp", [128, 4 * 512], F16)
        w2q = _sbB("w2q", [128, 8 * 1024], F16)
        y1s0 = _sbB("y1s0", [128, 8 * 512], F16)
        y1s1 = _sbB("y1s1", [128, 8 * 512], F16)
        qraw2 = _sbB("qraw2", [128, 2 * 1024], F32)
        srep2 = _sbB("srep2", [128, 2 * 1024], F32)
        psS0 = _psB("psS0", [128, 1024], F32)
        psS1 = _psB("psS1", [128, 1024], F32)
        psF0 = _psB("psF0", [128, 512], F32)
        psF1 = _psB("psF1", [128, 512], F32)
        psF2 = _psB("psF2", [128, 512], F32)
        psF3 = _psB("psF3", [128, 512], F32)
        with nc.Block("B") as blk:
            psS = [psS0, psS1]
            psF = [psF0, psF1, psF2, psF3]
            y1s = [y1s0, y1s1]

            def rcv1_slice(gi2):
                k2, cc = gi2 // 2, gi2 % 2
                r0 = 256 * k2 + 128 * cc
                return rcv1.ap()[r0:r0 + 128, :]

            def snd2_slice2(gi2, half):
                k2, cc = gi2 // 2, gi2 % 2
                r0 = 256 * k2 + 128 * cc
                c0 = 1024 * half
                return snd2.ap()[r0:r0 + 128, c0:c0 + 1024]

            def b_sync(e):
                e.wait_ge(bar, 5)
                for gi2 in range(16):
                    if gi2 == 0:
                        e.wait_ge(cc1, 1)
                    if gi2 >= 4:
                        e.wait_ge(sB, gi2 - 3)
                    e.dma_start(out=xcvs[:, (gi2 % 4) * 2048:(gi2 % 4 + 1) * 2048],
                                in_=rcv1_slice(gi2)).then_inc(dBS[gi2 % 4], 16)
                e.nop().then_inc(bar, 1)

            def b_scalar(s):
                s.wait_ge(bar, 5)
                for gi2 in range(16):
                    cc = gi2 % 2
                    xc = xcvs[:, (gi2 % 4) * 2048:(gi2 % 4 + 1) * 2048]
                    s.wait_ge(dBS[gi2 % 4], 16 * (gi2 // 4 + 1))
                    s.activation(out=e32[:], in_=xc, func=AF.Exp)
                    if gi2 >= 4:
                        s.wait_ge(vS, 16 * (gi2 - 3))
                    s.activation(out=bx16[:, (gi2 % 4) * 2048:(gi2 % 4 + 1) * 2048],
                                 in_=xc, func=AF.Square).then_inc(sB, 1)
                    s.drain()
                    s.activation(out=delta32[:], in_=e32[:], func=AF.Ln, bias=1.0)
                    s.drain()
                    for n in range(NST):
                        idx = 16 * gi2 + n
                        w = idx - 7
                        if w >= 1:
                            s.wait_ge(vS, w)
                        s.activation(out=ad16[:, (n % 8) * 2048:(n % 8 + 1) * 2048],
                                     in_=delta32[:], func=AF.Exp,
                                     scale=a32[:, cc * NST + n:cc * NST + n + 1]).then_inc(sAD, 1)
                        if idx >= 8:
                            pidx = idx - 8
                            pch = ((pidx // 16) % 2) * 16 + (pidx % 16)
                            s.wait_ge(vS, pidx + 1)
                            s.copy(out=carry[:, pch:pch + 1],
                                   in_=hring[:, (pidx % 8) * 2048 + 2047:(pidx % 8) * 2048 + 2048]).then_inc(sCR, 1)
                        if n == 7 and gi2 >= 1:
                            j = gi2 - 1
                            s.wait_ge(tSG, 16 * gi2)
                            for half in range(2):
                                if j >= 1:
                                    s.wait_ge(gd2S[half], 16 * j)
                                s.copy(out=hs16[:, half * 1024:(half + 1) * 1024],
                                       in_=psS[half][:, :]).then_inc(sHS, 1)
                        if 1 <= gi2 <= 8:
                            zj = 16 * (gi2 - 1) + n
                            q, ft, oc = zj // 32, (zj // 8) % 4, zj % 8
                            if zj % 4 == 0:
                                s.wait_ge(tF2, zj + 4)
                            if q == 0:
                                s.copy(out=z16[:, oc * 2048 + ft * 512:oc * 2048 + (ft + 1) * 512],
                                       in_=psF[zj % 4][:, :]).then_inc(sZ, 1)
                            else:
                                m = zj - 32 + 1
                                if m >= 5:
                                    s.wait_ge(gZ, m - 4)
                                s.activation(out=ztmp[:, (zj % 4) * 512:(zj % 4 + 1) * 512],
                                             in_=psF[zj % 4][:, :],
                                             func=AF.Copy).then_inc(sZ, 1)
                for pidx in range(248, 256):
                    pch = ((pidx // 16) % 2) * 16 + (pidx % 16)
                    s.wait_ge(vS, pidx + 1)
                    s.copy(out=carry[:, pch:pch + 1],
                           in_=hring[:, (pidx % 8) * 2048 + 2047:(pidx % 8) * 2048 + 2048]).then_inc(sCR, 1)
                s.wait_ge(tSG, 256)
                for half in range(2):
                    s.wait_ge(gd2S[half], 16 * 15)
                    s.copy(out=hs16[:, half * 1024:(half + 1) * 1024],
                           in_=psS[half][:, :]).then_inc(sHS, 1)
                s.nop().then_inc(bar, 1)

            def b_vector(v):
                v.wait_ge(bar, 5)
                for gi2 in range(16):
                    cc = gi2 % 2
                    v.wait_ge(sB, gi2 + 1)
                    for n in range(NST):
                        idx = 16 * gi2 + n
                        ch = cc * 16 + n
                        v.wait_ge(sAD, idx + 1)
                        if idx >= 8:
                            v.wait_ge(tSG, idx - 7)
                            v.wait_ge(sCR, idx - 7)
                        hb = hring[:, (idx % 8) * 2048:(idx % 8 + 1) * 2048]
                        init = zz_sb[:, 0:1] if gi2 < 2 else carry[:, ch:ch + 1]
                        v.tensor_tensor_scan(out=hb,
                                             data0=ad16[:, (n % 8) * 2048:(n % 8 + 1) * 2048],
                                             data1=bx16[:, (gi2 % 4) * 2048:(gi2 % 4 + 1) * 2048],
                                             initial=init, op0=AT.mult,
                                             op1=AT.add).then_inc(vS, 1)
                v.engine_nop().then_inc(bar, 1)

            def b_tensor(t):
                t.wait_ge(bar, 5)

                def fc2_jy(jy):
                    q, ft = jy // 4, jy % 4
                    t.wait_ge(dYS2[jy % 2], 16 * (jy // 2 + 1))
                    if jy % 4 == 0:
                        t.wait_ge(gW2, q + 1)
                    for oc in range(8):
                        zj = jy * 8 + oc
                        if zj >= 4:
                            t.wait_ge(sZ, zj - 3)
                        for fc in range(8):
                            ins = t.matmul(out=psF[zj % 4][:, :],
                                           lhsT=w2q[:, fc * 1024 + oc * 128:fc * 1024 + (oc + 1) * 128],
                                           rhs=y1s[jy % 2][:, fc * 512:(fc + 1) * 512],
                                           start=(fc == 0), stop=(fc == 7))
                        ins.then_inc(tF2, 1)

                for gi2 in range(16):
                    if 1 <= gi2 <= 8:
                        fc2_jy(2 * (gi2 - 1))
                        fc2_jy(2 * (gi2 - 1) + 1)
                    t.wait_ge(sHS, 2 * gi2)
                    for n in range(NST):
                        idx = 16 * gi2 + n
                        t.wait_ge(vS, idx + 1)
                        hb = hring[:, (idx % 8) * 2048:(idx % 8 + 1) * 2048]
                        t.matmul(out=psS0[:, 0:512], lhsT=eye_sb[:], rhs=hb[:, 0:512],
                                 start=(n == 0), stop=(n == 15))
                        t.matmul(out=psS0[:, 512:1024], lhsT=eye_sb[:], rhs=hb[:, 512:1024],
                                 start=(n == 0), stop=(n == 15))
                        t.matmul(out=psS1[:, 0:512], lhsT=eye_sb[:], rhs=hb[:, 1024:1536],
                                 start=(n == 0), stop=(n == 15))
                        ins = t.matmul(out=psS1[:, 512:1024], lhsT=eye_sb[:], rhs=hb[:, 1536:2048],
                                       start=(n == 0), stop=(n == 15))
                        ins.then_inc(tSG, 1)
                t.nop().then_inc(bar, 1)

            def b_gpsimd(g):
                g.wait_ge(bar, 5)

                def dequant_quarter(q):
                    if q >= 1:
                        g.wait_ge(tF2, 32 * q)
                    for kk in range(8):
                        sl = kk % 2
                        if kk == 0:
                            for pre in range(2):
                                g.dma_start(out=qraw2[:, pre * 1024:(pre + 1) * 1024],
                                            in_=q2[1024 * q + 128 * pre:1024 * q + 128 * (pre + 1), :]).then_inc(gq2S[pre], 16)
                                g.dma_start(out=srep2[:, pre * 1024:(pre + 1) * 1024],
                                            in_=s2r[1024 * q + 128 * pre:1024 * q + 128 * (pre + 1), :]).then_inc(gq2S[pre], 16)
                        g.wait_ge(gq2S[sl], 32 * (q * 4 + kk // 2 + 1))
                        qr = qraw2[:, sl * 1024:(sl + 1) * 1024]
                        g.tensor_tensor(out=qr, in0=qr, in1=c128[:], op=AT.subtract)
                        g.drain()
                        g.tensor_tensor(out=w2q[:, kk * 1024:(kk + 1) * 1024],
                                        in0=qr, in1=srep2[:, sl * 1024:(sl + 1) * 1024], op=AT.mult)
                        g.drain()
                        if kk + 2 < 8:
                            g.dma_start(out=qraw2[:, sl * 1024:(sl + 1) * 1024],
                                        in_=q2[1024 * q + 128 * (kk + 2):1024 * q + 128 * (kk + 3), :]).then_inc(gq2S[sl], 16)
                            g.dma_start(out=srep2[:, sl * 1024:(sl + 1) * 1024],
                                        in_=s2r[1024 * q + 128 * (kk + 2):1024 * q + 128 * (kk + 3), :]).then_inc(gq2S[sl], 16)
                    g.nop().then_inc(gW2, 1)

                def y1_stream(j):
                    q, ft = j // 4, j % 4
                    if j >= 2:
                        g.wait_ge(tF2, 8 * (j - 1))
                    g.dma_start(
                        out=y1s[j % 2].ap().rearrange("p (fc t) -> p fc t", fc=8),
                        in_=y1scr.ap()[1024 * q:1024 * (q + 1), 512 * ft:512 * (ft + 1)]
                            .rearrange("(fc p) t -> p fc t", p=128),
                    ).then_inc(dYS2[j % 2], 16)

                dequant_quarter(0)
                y1_stream(0)
                y1_stream(1)
                for gi2 in range(16):
                    for zj in (range(16 * (gi2 - 1), 16 * (gi2 - 1) + 16) if 1 <= gi2 <= 8 else []):
                        if zj >= 32:
                            g.wait_ge(sZ, zj + 1)
                            zsl = z16[:, (zj % 8) * 2048 + ((zj // 8) % 4) * 512:(zj % 8) * 2048 + ((zj // 8) % 4) * 512 + 512]
                            g.tensor_tensor(out=zsl, in0=zsl,
                                            in1=ztmp[:, (zj % 4) * 512:(zj % 4 + 1) * 512],
                                            op=AT.add)
                            g.drain()
                            g.nop().then_inc(gZ, 1)
                    for jj in (2 * gi2, 2 * gi2 + 1):
                        if 2 <= jj < 16:
                            y1_stream(jj)
                    if gi2 >= 1:
                        j = gi2 - 1
                        for half in range(2):
                            g.wait_ge(sHS, 2 * j + half + 1)
                            g.dma_start(out=snd2_slice2(j, half),
                                        in_=hs16[:, half * 1024:(half + 1) * 1024]).then_inc(gd2S[half], 16)
                    if gi2 in (2, 4, 6):
                        dequant_quarter(gi2 // 2)
                for half in range(2):
                    g.wait_ge(sHS, 2 * 15 + half + 1)
                    g.dma_start(out=snd2_slice2(15, half),
                                in_=hs16[:, half * 1024:(half + 1) * 1024]).then_inc(gd2S[half], 16)
                g.wait_ge(gd2S[0], 16 * 16)
                g.wait_ge(gd2S[1], 16 * 16)
                with allow_ici_all_to_all():
                    g.collective_compute(
                        "AllToAll", AT.bypass, replica_groups=[list(range(NC))],
                        ins=[snd2.ap().opt()], outs=[rcv2.ap().opt()],
                    ).then_inc(cc2, 1)
                g.nop().then_inc(bar, 1)

            blk.sync(b_sync)
            blk.scalar(b_scalar)
            blk.vector(b_vector)
            blk.tensor(b_tensor)
            blk.gpsimd(b_gpsimd)

        _stB.close()

        _stC = contextlib.ExitStack()

        def _sbC(name, shape, dt):
            return _stC.enter_context(nc.sbuf_tensor(name, shape, dt))

        def _psC(name, shape, dt):
            return _stC.enter_context(nc.psum_tensor(name, shape, dt))

        hsC = _sbC("hsC", [128, 2 * 2048], F16)
        xcvC = _sbC("xcvC", [128, 2 * 2048], F16)
        uC = _sbC("uC", [128, 2 * 2048], F16)
        tmp16 = _sbC("tmp16", [128, 2048], F16)
        y16f = _sbC("y16f", [128, 16 * 2048], F16)
        woT = _sbC("woT", [128, 16 * 1024], F16)
        a16t = _sbC("a16t", [128, 2 * 512], F16)
        atm = _sbC("atm", [128, 4 * 1024], F16)
        ztm = _sbC("ztm", [128, 4 * 1024], F16)
        o1 = _sbC("o1", [128, 1024], F32)
        o2 = _sbC("o2", [128, 1024], F32)
        outt = _sbC("outt", [128, 2 * 1024], F32)
        xtC = _sbC("xtC", [128, 2 * D], F32)
        psW0 = _psC("psW0", [128, 512], F32)
        psW1 = _psC("psW1", [128, 512], F32)
        psX0 = _psC("psX0", [128, 512], F16)
        psX1 = _psC("psX1", [128, 512], F16)
        with nc.Block("C") as blkC:
            psW = [psW0, psW1]

            def xtile_c(ft):
                return xtC[:, (ft % 2) * D:(ft % 2 + 1) * D]

            def c_sync(e):
                e.wait_ge(bar, 10)
                for cc in range(16):
                    if cc == 0:
                        e.wait_ge(cc2, 1)
                    if cc >= 2:
                        e.wait_ge(vY, cc - 1)
                    e.dma_start(out=hsC[:, (cc % 2) * 2048:(cc % 2 + 1) * 2048],
                                in_=rcv2.ap()[128 * cc:128 * (cc + 1), :]).then_inc(dCS[cc % 2], 16)
                    e.dma_start(out=xcvC[:, (cc % 2) * 2048:(cc % 2 + 1) * 2048],
                                in_=snd1.ap()[128 * cc:128 * (cc + 1), :]).then_inc(dCS[cc % 2], 16)
                    e.dma_start(out=uC[:, (cc % 2) * 2048:(cc % 2 + 1) * 2048],
                                in_=uscr.ap()[128 * cc:128 * (cc + 1), :]).then_inc(dCS[cc % 2], 16)
                e.nop().then_inc(bar, 1)

            def c_gpsimd(g):
                g.wait_ge(bar, 10)
                g.dma_start(out=woT.ap().rearrange("p (cc o) -> p cc o", cc=16),
                            in_=woutt.rearrange("(cc p) o -> p cc o", p=128)).then_inc(gWo, 16)
                g.dma_start(out=dvec_sb[:], in_=dvec).then_inc(gWo, 16)

                g.dma_start(out=xtile_c(0), in_=xs[128 * 1:128 * 2, :]).then_inc(dxCS[0], 16)
                g.dma_start(out=xtile_c(1), in_=xs[128 * 2:128 * 3, :]).then_inc(dxCS[1], 16)
                for ft in range(16):
                    g.wait_ge(vOut, ft + 1)
                    g.dma_start(out=out[128 * ft:128 * (ft + 1), :],
                                in_=outt[:, (ft % 2) * D:(ft % 2 + 1) * D]).then_inc(dOutS[ft % 2], 16)
                    if ft + 2 < 16:
                        g.dma_start(out=xtile_c(ft + 2),
                                    in_=xs[128 * (ft + 3):128 * (ft + 4), :]).then_inc(dxCS[ft % 2], 16)
                if debug:
                    g.dma_start(out=dbg["dbg_snd1"], in_=snd1.ap()).then_inc(dOut, 16)
                    g.dma_start(out=dbg["dbg_rcv2"], in_=rcv2.ap()).then_inc(dOut, 16)
                    g.dma_start(out=dbg["dbg_y1"], in_=y1scr.ap()).then_inc(dOut, 16)
                    g.dma_start(out=dbg["dbg_u"], in_=uscr.ap()).then_inc(dOut, 16)
                g.wait_ge(dOutS[0], 16 * 8)
                g.wait_ge(dOutS[1], 16 * 8)
                if debug:
                    g.wait_ge(dOut, 16 * 4)

            def c_vector(v):
                v.wait_ge(bar, 10)
                v.wait_ge(sZ, 128)
                v.wait_ge(gZ, 96)
                for oc in range(8):
                    v.tensor_scalar(out=z16[:, oc * 2048:(oc + 1) * 2048],
                                    in0=z16[:, oc * 2048:(oc + 1) * 2048],
                                    scalar1=b2_sb[:, oc:oc + 1], scalar2=None,
                                    op0=AT.add)
                    v.drain()
                    v.engine_nop().then_inc(gZ, 1)
                v.wait_ge(gWo, 32)
                v.drain()
                v.tensor_scalar(out=r1p1[:], in0=r1all[:], scalar1=1.0,
                                scalar2=None, op0=AT.add)
                v.drain()
                for cc in range(16):
                    v.wait_ge(dCS[cc % 2], 48 * (cc // 2 + 1))
                    v.tensor_tensor(out=tmp16[:],
                                    in0=uC[:, (cc % 2) * 2048:(cc % 2 + 1) * 2048],
                                    in1=hsC[:, (cc % 2) * 2048:(cc % 2 + 1) * 2048],
                                    op=AT.mult)
                    v.drain()
                    v.scalar_tensor_tensor(out=y16f[:, cc * 2048:(cc + 1) * 2048],
                                           in0=xcvC[:, (cc % 2) * 2048:(cc % 2 + 1) * 2048],
                                           scalar=dvec_sb[:, cc:cc + 1],
                                           in1=tmp16[:], op0=AT.mult,
                                           op1=AT.add).then_inc(vY, 1)
                    v.drain()
                for ft in range(16):
                    tt, ii = ft // 4, ft % 4
                    v.wait_ge(dxCS[ft % 2], 16 * (ft // 2 + 1))
                    v.wait_ge(sTr, 2 * (8 * tt + 8))
                    if ft >= 2:
                        v.wait_ge(dOutS[ft % 2], 16 * (ft // 2))
                    v.tensor_scalar(out=o1[:], in0=xtile_c(ft),
                                    scalar1=r1p1[:, ft + 1:ft + 2],
                                    scalar2=nmr1all[:, ft + 1:ft + 2],
                                    op0=AT.mult, op1=AT.add)
                    v.scalar_tensor_tensor(out=o2[:], in0=ztm[:, ii * 1024:(ii + 1) * 1024],
                                           scalar=0.5, in1=atm[:, ii * 1024:(ii + 1) * 1024],
                                           op0=AT.mult, op1=AT.add)
                    v.drain()
                    v.tensor_tensor(out=outt[:, (ft % 2) * D:(ft % 2 + 1) * D],
                                    in0=o1[:], in1=o2[:], op=AT.add).then_inc(vOut, 1)
                    v.drain()

            def c_tensor(t):
                t.wait_ge(bar, 10)
                t.wait_ge(gWo, 32)

                def wout_group(wi):
                    tt, oc = wi // 8, wi % 8
                    if wi >= 2:
                        t.wait_ge(sW, wi - 1)
                    for cc in range(16):
                        if wi == 0:
                            t.wait_ge(vY, cc + 1)
                        ins = t.matmul(out=psW[wi % 2][:, :],
                                       lhsT=woT[:, cc * 1024 + oc * 128:cc * 1024 + (oc + 1) * 128],
                                       rhs=y16f[:, cc * 2048 + tt * 512:cc * 2048 + (tt + 1) * 512],
                                       start=(cc == 0), stop=(cc == 15))
                    ins.then_inc(tW, 1)

                def transpose_pair(wi):
                    tt, oc = wi // 8, wi % 8
                    t.wait_ge(sW, wi + 1)
                    if wi >= 1:
                        t.wait_ge(sTr, 2 * wi - 1)
                    if tt >= 1 and oc == 0:
                        t.wait_ge(vOut, 4 * tt)
                    for ii in range(4):
                        t.transpose(out=psX0[:, ii * 128:(ii + 1) * 128],
                                    in_=a16t[:, (wi % 2) * 512 + ii * 128:(wi % 2) * 512 + (ii + 1) * 128],
                                    identity=eye_sb[:])
                    t.drain()
                    t.nop().then_inc(tTr, 1)
                    t.wait_ge(gZ, 96 + oc + 1)
                    for ii in range(4):
                        t.transpose(out=psX1[:, ii * 128:(ii + 1) * 128],
                                    in_=z16[:, oc * 2048 + tt * 512 + ii * 128:oc * 2048 + tt * 512 + (ii + 1) * 128],
                                    identity=eye_sb[:])
                    t.drain()
                    t.nop().then_inc(tTr, 1)

                for wi in range(32):
                    wout_group(wi)
                    if wi >= 1:
                        transpose_pair(wi - 1)
                transpose_pair(31)

            def c_scalar(s):
                s.wait_ge(bar, 10)
                for wi in range(32):
                    tt, oc = wi // 8, wi % 8
                    s.wait_ge(tW, wi + 1)
                    if wi >= 2:
                        s.wait_ge(tTr, 2 * (wi - 2) + 1)
                    s.copy(out=a16t[:, (wi % 2) * 512:(wi % 2 + 1) * 512],
                           in_=psW[wi % 2][:, :]).then_inc(sW, 1)
                    s.wait_ge(tTr, 2 * wi + 1)
                    for ii in range(4):
                        s.copy(out=atm[:, ii * 1024 + oc * 128:ii * 1024 + (oc + 1) * 128],
                               in_=psX0[:, ii * 128:(ii + 1) * 128])
                    s.drain()
                    s.nop().then_inc(sTr, 1)
                    s.wait_ge(tTr, 2 * wi + 2)
                    for ii in range(4):
                        s.copy(out=ztm[:, ii * 1024 + oc * 128:ii * 1024 + (oc + 1) * 128],
                               in_=psX1[:, ii * 128:(ii + 1) * 128])
                    s.drain()
                    s.nop().then_inc(sTr, 1)


            blkC.sync(c_sync)
            blkC.gpsimd(c_gpsimd)
            blkC.vector(c_vector)
            blkC.tensor(c_tensor)
            blkC.scalar(c_scalar)
        _stC.close()
        _stZ.close()

    return nc


def kernel(**inputs):
    x = np.asarray(inputs["L0_attn_x"])[0]
    W_in = np.asarray(inputs["L0_attn_W_in"])
    conv_w = np.asarray(inputs["L0_attn_conv_w"])
    conv_b = np.asarray(inputs["L0_attn_conv_b"])
    A_log = np.asarray(inputs["L0_attn_A_log"])
    Dv = np.asarray(inputs["L0_attn_D"])
    W_out = np.asarray(inputs["L0_attn_W_out"])
    g = np.asarray(inputs["L0_ffn_ln_gamma"])
    b = np.asarray(inputs["L0_ffn_ln_beta"])
    fc1_q = np.asarray(inputs["L0_ffn_fc1_q"])
    fc1_s = np.asarray(inputs["L0_ffn_fc1_scales"])
    fc1_b = np.asarray(inputs["L0_ffn_fc1_bias"])
    fc2_q = np.asarray(inputs["L0_ffn_fc2_q"])
    fc2_s = np.asarray(inputs["L0_ffn_fc2_scales"])
    fc2_b = np.asarray(inputs["L0_ffn_fc2_bias"])

    xpad = np.concatenate([np.zeros((128, D), np.float32), x], axis=0)
    wint = np.ascontiguousarray(W_in.T)
    woutt = np.ascontiguousarray(W_out.T)
    convw = np.ascontiguousarray(conv_w[:, 0, :].reshape(16, 128, 4).transpose(1, 0, 2).reshape(128, 64))
    convb = np.ascontiguousarray(conv_b.reshape(16, 128).T)
    dvec = np.ascontiguousarray(Dv.reshape(16, 128).T)
    gam = np.ascontiguousarray(np.broadcast_to(g, (128, D)))
    bet = np.ascontiguousarray(np.broadcast_to(b, (128, D)))
    s1r = np.ascontiguousarray(np.repeat(fc1_s, 128, axis=0))
    b1m = np.ascontiguousarray(fc1_b.reshape(32, 128).T)
    s2r = np.ascontiguousarray(np.repeat(fc2_s, 128, axis=0))
    b2m = np.ascontiguousarray(fc2_b.reshape(8, 128).T)
    eyem = np.eye(128, dtype=np.float16)
    zzm = np.zeros((128, 1), np.float32)
    cstm = np.ascontiguousarray(
        np.broadcast_to(np.array([EPS, 1.0], np.float32), (128, 2)))
    alog_r = A_log.reshape(16, 128, NST)

    nc = build_nc(debug=False)
    in_maps = []
    for c in range(NC):
        al = np.ascontiguousarray(
            np.concatenate([alog_r[2 * c], alog_r[2 * c + 1]], axis=1))
        in_maps.append({
            "xs": np.ascontiguousarray(xpad[TLOC * c:TLOC * c + TEXT]),
            "wint": wint, "woutt": woutt, "convw": convw, "convb": convb,
            "alog": al, "dvec": dvec, "gam": gam, "bet": bet,
            "q1": fc1_q, "s1r": s1r, "b1": b1m,
            "q2": fc2_q, "s2r": s2r, "b2": b2m,
            "eye": eyem, "zz": zzm, "cst": cstm,
        })
    res = run_bass_kernel_spmd(nc, in_maps, core_ids=list(range(NC)))
    full = np.concatenate([np.asarray(res.results[c]["out"]) for c in range(NC)], axis=0)
    return full.reshape(1, 16384, D).astype(np.float32)
